# version 04
# baseline (speedup 1.0000x reference)
import numpy as np
import ml_dtypes
from contextlib import ExitStack

import concourse.bass as bass
import concourse.mybir as mybir
from concourse.bass_utils import run_bass_kernel_spmd

F32 = mybir.dt.float32
BF16 = mybir.dt.bfloat16
I32 = mybir.dt.int32
U8 = mybir.dt.uint8
AF = mybir.ActivationFunctionType
ALU = mybir.AluOpType
AX = mybir.AxisListType

P = 128
T = 2048
C = 2048
H, N = 32, 64
L = 64
NCH = T // L
KC = C // P
TT = T // P
HID = 1792
HC = HID // P
NEXP = 8
DMIX, DDEC = 32, 64
GN_EPS = 1e-5 * 64.0
LN_EPS = 1e-5

ENGINES = ("sync", "tensor", "vector", "scalar", "gpsimd")
SB_BYTES = 200 * 1024
PS_F32 = 4096


class Mem:
    def __init__(self, total=SB_BYTES):
        self.free_list = [(0, total)]
        self.regions = {}

    def alloc(self, name, nbytes, align=64):
        assert name not in self.regions, name
        if not hasattr(self, "log"):
            self.log = {}
        nbytes = -(-nbytes // align) * align
        for i, (off, sz) in enumerate(self.free_list):
            aoff = -(-off // align) * align
            pad = aoff - off
            if sz >= nbytes + pad:
                rest = []
                if pad:
                    rest.append((off, pad))
                if sz - pad - nbytes:
                    rest.append((aoff + nbytes, sz - pad - nbytes))
                self.free_list[i:i + 1] = rest
                self.regions[name] = (aoff, nbytes)
                self.log[name] = (aoff, nbytes)
                return aoff
        raise MemoryError(f"SBUF arena full for {name} ({nbytes}B): {self.regions}")

    def free(self, *names):
        for name in names:
            off, sz = self.regions.pop(name)
            self.free_list.append((off, sz))
        self.free_list.sort()
        merged = []
        for off, sz in self.free_list:
            if merged and merged[-1][0] + merged[-1][1] == off:
                merged[-1] = (merged[-1][0], merged[-1][1] + sz)
            else:
                merged.append((off, sz))
        self.free_list = merged


class Prog:

    def __init__(self, nc, ctx):
        self.nc = nc
        self.ctx = ctx
        self.q = {e: [] for e in ENGINES}
        self.semh = {}
        self.cnt = {}

    def sem(self, *names):
        for name in names:
            if name in self.semh:
                continue
            self.semh[name] = self.ctx.enter_context(self.nc.semaphore(name))
            self.cnt[name] = 0

    def emit(self, eng, fn, wait=(), inc=None, n=1):
        wl = []
        for s, v in wait:
            v = int(v)
            assert v <= self.cnt[s], (s, v, self.cnt[s])
            if v > 0:
                wl.append((self.semh[s], v))
        ih = self.semh[inc] if inc else None

        def thunk(E, wl=wl, fn=fn, ih=ih, n=n):
            for h, v in wl:
                E.wait_ge(h, v)
            ins = fn(E)
            if ih is not None:
                ins.then_inc(ih, n)
        self.q[eng].append(thunk)
        if inc:
            self.cnt[inc] += n
            return (inc, self.cnt[inc])

    def dma(self, eng, sem, out, in_, wait=(), **kw):
        self.sem(sem)
        return self.emit(eng, lambda E, o=out, i=in_, kw=kw: E.dma_start(out=o, in_=i, **kw),
                         wait, sem, 16)

    def mm(self, out, lhsT, rhs, start=True, stop=True, wait=(), tile_position=None):
        return self.emit(
            "tensor",
            lambda E, o=out, l=lhsT, r=rhs, s=start, p=stop, tp=tile_position:
                E.matmul(o, l, r, start=s, stop=p, tile_position=tp),
            wait, "mm", 1)

    def trans(self, out, in_, ident, wait=()):
        return self.emit("tensor",
                         lambda E, o=out, i=in_, d=ident: E.transpose(o, i, d),
                         wait, "mm", 1)

    def act(self, out, in_, func, bias=0.0, scale=1.0, accum_out=None, wait=()):
        wait = list(wait) + [("ea", self.cnt["ea"])]
        return self.emit(
            "scalar",
            lambda E, o=out, i=in_, f=func, b=bias, s=scale, a=accum_out:
                E.activation(o, i, f, bias=b, scale=s, accum_out=a),
            wait, "ea", 1)

    def ev(self, eng, name, *args, wait=(), **kw):
        inc = {"vector": "ev", "scalar": "ea", "gpsimd": "pg", "tensor": "mm"}[eng]
        if eng != "tensor":
            wait = list(wait) + [(inc, self.cnt[inc])]
        return self.emit(eng,
                         lambda E, nm=name, a=args, k=kw: getattr(E, nm)(*a, **k),
                         wait, inc, 1)

    def now(self, s):
        self.sem(s)
        return (s, self.cnt[s])

    def fence(self, engs=ENGINES, sems=None):
        if sems is None:
            sems = list(self.cnt)
        targets = [(self.semh[s], self.cnt[s]) for s in sems if self.cnt[s] > 0]
        for e in engs:
            def thunk(E, wl=list(targets)):
                for h, v in wl:
                    E.wait_ge(h, v)
            self.q[e].append(thunk)

    def finalize(self, block):
        for eng in ENGINES:
            thunks = self.q[eng]
            if not thunks:
                continue
            dec = getattr(block, eng)

            def body(E, thunks=thunks):
                for t in thunks:
                    t(E)
            dec(body)


def _wl(*tokens):
    return [t for t in tokens if t is not None]


_LAST_MEM = [None]


def build_program(debug=False, upto=99, races=True, exch=True):
    nc = bass.Bass("TRN2", target_bir_lowering=False,
                   detect_race_conditions=races)

    def din(name, shape, dtype=F32):
        return nc.dram_tensor(name, list(shape), dtype, kind="ExternalInput").ap()

    x_in = din("x", [T, C])
    xprev_in = din("xprev", [P, KC])
    w1_in = din("tm_w1", [C, 160])
    w2_in = din("tm_w2", [5, DMIX, C])
    tdw1_in = din("td_w1", [C, DDEC])
    tdw2_in = din("td_w2", [DDEC, C])
    proj_w_in = {"k": din("wk", [C, C]), "v": din("wv", [C, C]),
                 "r": din("wr", [C, C]), "g": din("wg", [C, C])}
    wo_in = din("wo", [C, C])
    gate_in = din("gate_w", [C, NEXP])
    fw1_in = din("fw1", [NEXP, C, HID])
    fw2_in = din("fw2", [NEXP, HID, C])
    fw3_in = din("fw3", [NEXP, C, HID])
    sw1_in = din("sw1", [C, HID])
    sw2_in = din("sw2", [HID, C])
    sw3_in = din("sw3", [C, HID])

    maatab_in = din("maatab", [P, 6, KC])
    urep_in = din("urep", [P, C])
    lnwrep_in = din("lnwrep", [P, C])
    lnbrep_in = din("lnbrep", [P, C])
    tdrep_in = din("tdrep", [P, C])
    ebrep_in = din("ebrep", [P, NEXP])
    l2i_in = din("l2i", [P, P])
    l2x_in = din("l2x", [P, P])
    l32x_in = din("l32x", [32, 32])
    masksu_in = din("masksu", [P, L])
    chones_in = din("chones", [P, 2])
    identf_in = din("identf", [P, P])
    identb_in = din("identb", [P, P])
    parity_in = din("parity", [1, 1], I32)
    smask_in = din("smask", [P, 1])
    nonce_in = din("nonce", [1, 1], I32)

    out_t = nc.dram_tensor("out", [T, C], F32, kind="ExternalOutput").ap()

    spk = "ExternalOutput" if debug else "Internal"

    def scratch(name, shape, dtype):
        return nc.dram_tensor(name, list(shape), dtype, kind=spk).ap()

    sp_xbf = scratch("sp_xbf", [T, C], BF16)
    sp_xf = scratch("sp_xf", [5, C, T], BF16)
    sp_rt = scratch("sp_rt", [T, C], BF16)
    sp_kt = scratch("sp_kt", [T, C], BF16)
    sp_v = scratch("sp_v", [T, C], BF16)
    sp_g = scratch("sp_g", [T, C], BF16)
    sp_b = scratch("sp_b", [T, C], BF16)
    sp_ap = scratch("sp_ap", [T, C], BF16)
    sp_d = scratch("sp_d", [T, C], BF16)
    sp_sr = scratch("sp_sr", [NCH, C], F32)
    sp_q = scratch("sp_q", [T, H], F32)
    sp_yng = scratch("sp_yng", [T, C], BF16)
    sp_gr = nc.dram_tensor("sp_gr", [8, T], BF16, kind=spk)
    sp_h = nc.dram_tensor("sp_h", [9, HID, T], BF16, kind="Internal").ap()
    sp_xa = scratch("sp_xa", [T, C], F32)
    sp_out2 = scratch("sp_out2", [T, C], F32)
    sh_s = nc.dram_tensor("sh_s", [2, P, KC * N], F32, addr_space="Shared")
    sp_sstub = nc.dram_tensor("sp_sstub", [P, KC * N], F32)
    sh_fl = nc.dram_tensor("sh_fl", [2, 16], I32, addr_space="Shared")

    FPROJ = {"w": 0, "k": 1, "v": 2, "r": 3, "g": 4}

    mem = Mem()
    _LAST_MEM[0] = mem
    with ExitStack() as ctx:
        arena = ctx.enter_context(nc.sbuf_tensor("arena", [P, SB_BYTES], U8)).ap()
        psum = ctx.enter_context(nc.psum_tensor("psA", [P, PS_F32], F32)).ap()
        psum_bf = psum.bitcast(BF16)

        def bank(i, rows=P, cols=512, dt=F32):
            if dt is F32:
                return psum[0:rows, i * 512:i * 512 + cols]
            return psum_bf[0:rows, i * 1024:i * 1024 + cols]

        def sb(name, shape, dtype, align=64):
            nb = int(np.prod(shape[1:])) * mybir.dt.size(dtype)
            off = mem.alloc(name, nb, align)
            flat = arena[0:shape[0], off:off + nb].bitcast(dtype)
            if len(shape) > 2:
                names = " ".join(f"d{i}" for i in range(1, len(shape)))
                flat = flat.rearrange(f"p ({names}) -> p {names}",
                                      **{f"d{i}": int(shape[i]) for i in range(1, len(shape))})
            return flat

        p = Prog(nc, ctx)
        p.sem("mm", "ev", "ea", "pg", "fl")

        def freef(*names):
            mem.free(*names)
            p.fence()

        maatab = sb("maatab", [P, 6, KC], F32)
        l2i = sb("l2i", [P, P], F32)
        l2x = sb("l2x", [P, P], F32)
        l32x = sb("l32x", [32, 32], F32)
        masksu = sb("masksu", [P, L], F32)
        chones = sb("chones", [P, 2], F32)
        identf = sb("identf", [P, P], F32)
        identb = sb("identb", [P, P], BF16)
        epsc = sb("epsc", [P, 2], F32)
        alfm = sb("alfm", [P, KC, NCH], F32)
        eofm = sb("eofm", [P, KC, NCH], BF16)
        q_all = sb("q_all", [P, TT, H], F32)

        for dst, src in ((maatab, maatab_in),
                         (l2i, l2i_in), (l2x, l2x_in), (l32x, l32x_in),
                         (masksu, masksu_in), (chones, chones_in),
                         (identf, identf_in)):
            p.dma("scalar", "c_con", dst, src)
        d_consts_c = p.now("c_con")
        d_consts_g = p.dma("gpsimd", "g_idb", identb, identb_in)
        p.ev("vector", "memset", epsc[:, 0:1], float(GN_EPS))
        ev_eps = p.ev("vector", "memset", epsc[:, 1:2], float(LN_EPS))

        x_fm = sb("x_fm", [P, KC, T], BF16)
        xx_fm = sb("xx_fm", [P, KC, T], BF16)
        xtk = sb("xtk", [P, 2, C], BF16)
        xprev_c = sb("xprev_c", [P, KC], BF16)

        d_xbf = p.dma("gpsimd", "g_xbf", sp_xbf, x_in)
        g_xp = p.dma("gpsimd", "g_xpv", xprev_c, xprev_in)

        mm_tile = {}
        evdone = {}
        for tt in range(TT):
            buf = tt % 2
            d = p.dma("sync", f"y_xtk{buf}", xtk[:, buf, :],
                      sp_xbf[tt * P:(tt + 1) * P, :],
                      wait=_wl(d_xbf, mm_tile.get(tt - 2)))
            for ic in range(KC):
                bk = ic % 2
                idx = tt * KC + ic
                m = p.trans(bank(bk, dt=BF16, cols=P), xtk[:, buf, ic * P:(ic + 1) * P],
                            identb,
                            wait=_wl(d if ic == 0 else None,
                                     d_consts_g if idx == 0 else None,
                                     evdone.get(idx - 2)))
                evdone[idx] = p.ev("vector", "tensor_copy",
                                   x_fm[:, ic, tt * P:(tt + 1) * P],
                                   bank(bk, dt=BF16, cols=P), wait=_wl(m))
            mm_tile[tt] = p.now("mm")
        ev_xload = p.now("ev")

        for ic in range(KC):
            p.ev("vector", "tensor_sub", xx_fm[:, ic, 1:T], x_fm[:, ic, 0:T - 1],
                 x_fm[:, ic, 1:T], wait=_wl(ev_xload if ic == 0 else None))
            p.ev("vector", "tensor_sub", xx_fm[:, ic, 0:1], xprev_c[:, ic:ic + 1],
                 x_fm[:, ic, 0:1], wait=_wl(g_xp if ic == 0 else None))
        ev_xx = p.now("ev")
        freef("xtk", "xprev_c")

        w1sb = sb("w1sb", [P, KC, 160], BF16)
        w2sb = sb("w2sb", [32, 5, C], BF16)
        g_w1 = p.dma("gpsimd", "g_w1s", w1sb, w1_in.rearrange("(kc p) m -> p kc m", p=P))
        g_w2 = p.dma("gpsimd", "g_w2s", w2sb, w2_in.rearrange("f d c -> d f c"))
        m5 = sb("m5", [32, 5, T], BF16)
        xxp = sb("xxp", [P, KC, 512], BF16)

        mm_mgrp = {}
        ea_mgrp = {}
        for tch in range(4):
            tc = slice(tch * 512, (tch + 1) * 512)
            for kc in range(KC):
                p.ev("vector", "scalar_tensor_tensor",
                     xxp[:, kc, :], xx_fm[:, kc, tc], maatab[:, 0, kc:kc + 1],
                     x_fm[:, kc, tc], ALU.mult, ALU.add,
                     wait=_wl(ev_xx if tch == 0 and kc == 0 else None,
                              mm_mgrp.get(tch - 1) if kc == 0 else None))
            evp = p.now("ev")
            bkA, bkB = 2 * (tch % 2), 2 * (tch % 2) + 1
            for kc in range(KC):
                p.mm(bank(bkA, rows=96), w1sb[:, kc, 0:96], xxp[:, kc, :],
                     start=(kc == 0), stop=(kc == KC - 1),
                     wait=_wl(evp if kc == 0 else None,
                              g_w1 if tch == 0 and kc == 0 else None,
                              ea_mgrp.get(tch - 2) if kc == 0 else None))
                p.mm(bank(bkB, rows=64), w1sb[:, kc, 96:160], xxp[:, kc, :],
                     start=(kc == 0), stop=(kc == KC - 1))
            mm_mgrp[tch] = p.now("mm")
            for f in range(5):
                if f < 3:
                    src = psum[(f % 3) * 32:(f % 3) * 32 + 32, bkA * 512:bkA * 512 + 512]
                else:
                    src = psum[(f - 3) * 32:(f - 3) * 32 + 32, bkB * 512:bkB * 512 + 512]
                p.act(m5[:, f, tc], src, AF.Tanh,
                      wait=_wl(mm_mgrp[tch] if f == 0 else None))
            ea_mgrp[tch] = p.now("ea")
        ea_m = p.now("ea")
        if upto < 2:
            alldma = [p.now(s) for s in list(p.cnt) if s[:2] in ("y_", "g_", "c_")]
            _tail(p, out_t, x_in, wait=_wl(p.now("ev"), p.now("ea"), p.now("mm"),
                                           *alldma))
            p.finalize(ctx.enter_context(nc.Block()))
            return nc
        freef("w1sb", "xxp")

        xfst = sb("xfst", [P, 8, 512], BF16)
        tmpf = sb("tmpf", [P, 4, 512], F32)
        pdone = {}
        pst = {}
        piece = 0
        for fname, f in sorted(FPROJ.items(), key=lambda kv: kv[1]):
            for kc in range(KC):
                for tch in range(4):
                    tc = slice(tch * 512, (tch + 1) * 512)
                    bk = 4 + piece % 4
                    mmx = p.mm(bank(bk), w2sb[:, f, kc * P:(kc + 1) * P],
                               m5[:, f, tc],
                               wait=_wl(ea_m if piece == 0 else None,
                                        g_w2 if piece == 0 else None,
                                        pdone.get(piece - 4)))
                    p.ev("vector", "scalar_tensor_tensor",
                         tmpf[:, piece % 4, :], bank(bk), maatab[:, 1 + f, kc:kc + 1],
                         xx_fm[:, kc, tc], ALU.add, ALU.mult,
                         wait=_wl(mmx, pdone.get(piece - 4)))
                    stt_t = p.now("ev")
                    pdone[piece] = p.ev("gpsimd", "tensor_add", xfst[:, piece % 8, :],
                                        tmpf[:, piece % 4, :], x_fm[:, kc, tc],
                                        wait=_wl(stt_t, pst.get(piece - 8)))
                    pst[piece] = p.dma(
                        "sync", f"y_xf{piece % 4}",
                        sp_xf[f, kc * P:(kc + 1) * P, tch * 512:(tch + 1) * 512],
                        xfst[:, piece % 8, :], wait=_wl(pdone[piece],
                                                        pst.get(piece - 4)))
                    piece += 1
        dy_xf = [p.now(f"y_xf{s}") for s in range(4)]
        freef("x_fm", "xx_fm", "m5", "w2sb", "xfst", "tmpf")

        if upto < 3:
            alldma = [p.now(s) for s in list(p.cnt) if s[:2] in ("y_", "g_", "c_")]
            _tail(p, out_t, x_in, wait=_wl(p.now("ev"), p.now("ea"), p.now("mm"),
                                           *alldma))
            p.finalize(ctx.enter_context(nc.Block()))
            return nc

        xfw = sb("xfw", [P, KC, T], BF16)
        tdw1 = sb("tdw1", [P, KC, DDEC], BF16)
        tdw2 = sb("tdw2", [64, C], BF16)
        tdrep = sb("tdrep", [P, C], F32)
        u1 = sb("u1", [64, T], BF16)
        e_tile = sb("e_tile", [P, 2, C], F32)
        dstage = sb("dstage", [P, 2, C], BF16)
        bstage = sb("bstage", [P, 2, C], BF16)
        apstage = sb("apstage", [P, 2, C], BF16)
        srstag = sb("srstag", [2, 2, C], F32)

        g_xfw = p.dma("gpsimd", "g_xfw", xfw,
                      sp_xf[0].rearrange("(kc p) t -> p kc t", p=P),
                      wait=_wl(*dy_xf))
        g_td1 = p.dma("gpsimd", "g_td1", tdw1,
                      tdw1_in.rearrange("(kc p) m -> p kc m", p=P))
        g_td2 = p.dma("gpsimd", "g_td2", tdw2, tdw2_in)
        c_tdr = p.dma("scalar", "c_tdr", tdrep, tdrep_in)

        ea_u1g = {}
        for tch in range(4):
            bk = tch % 2
            for kc in range(KC):
                p.mm(bank(bk, rows=64), tdw1[:, kc, :],
                     xfw[:, kc, tch * 512:(tch + 1) * 512],
                     start=(kc == 0), stop=(kc == KC - 1),
                     wait=_wl(g_td1 if tch == 0 and kc == 0 else None,
                              g_xfw if tch == 0 and kc == 0 else None,
                              ea_u1g.get(tch - 2) if kc == 0 else None))
            g = p.now("mm")
            ea_u1g[tch] = p.act(u1[0:64, tch * 512:(tch + 1) * 512], bank(bk, rows=64),
                                AF.Tanh, wait=_wl(g))
        ea_u1 = p.now("ea")

        edone = {}
        dsp_d, dsp_b, dsp_a = {}, {}, {}
        wgrp = {}
        eadec = {}
        evsr = {}
        dydec = {}
        mmcs = {}
        for tt in range(TT):
            dbuf = tt % 2
            for nch in range(4):
                ncs = slice(nch * 512, (nch + 1) * 512)
                bk = 2 + (nch % 2)
                wgrp[(tt, nch)] = p.mm(
                    bank(bk), u1[0:64, tt * P:(tt + 1) * P], tdw2[0:64, ncs],
                    wait=_wl(g_td2 if tt == 0 and nch == 0 else None,
                             ea_u1 if tt == 0 and nch == 0 else None,
                             edone.get((tt, nch - 2)) or edone.get((tt - 1, 2 + nch))))
                edone[(tt, nch)] = p.ev(
                    "vector", "tensor_add", e_tile[:, tt % 2, ncs], bank(bk),
                    tdrep[:, ncs],
                    wait=_wl(wgrp[(tt, nch)],
                             c_tdr if tt == 0 and nch == 0 else None,
                             mmcs.get(tt - 2) if nch == 0 else None,
                             dydec.get(tt - 2) if nch == 0 else None,
                             eadec.get(tt - 2) if nch == 0 else None))
                p.act(e_tile[:, tt % 2, ncs], e_tile[:, tt % 2, ncs], AF.Exp,
                      wait=_wl(edone[(tt, nch)]))
                p.act(dstage[:, dbuf, ncs], e_tile[:, tt % 2, ncs], AF.Exp,
                      scale=-1.0,
                      wait=_wl(dsp_d.get(tt - 2) if nch == 0 else None))
            ev_et = p.now("ea")
            pe_ea = pe_ev = None
            for nch in range(4):
                ncs = slice(nch * 512, (nch + 1) * 512)
                g1 = p.mm(bank(4), l2i, e_tile[:, tt % 2, ncs],
                          wait=_wl(ev_et if nch == 0 else None,
                                   eadec.get(tt - 1) if nch == 0 else None,
                                   evsr.get(tt - 1) if nch == 0 else None,
                                   pe_ea, pe_ev))
                g2 = p.mm(bank(5), l2x, e_tile[:, tt % 2, ncs])
                g3 = p.mm(psum[0:2, 6 * 512:6 * 512 + 512], chones,
                          e_tile[:, tt % 2, ncs])
                p.act(bstage[:, dbuf, ncs], bank(4), AF.Exp,
                      wait=_wl(g3, dsp_b.get(tt - 2) if nch == 0 else None,
                               dsp_a.get(tt - 2) if nch == 0 else None))
                p.act(apstage[:, dbuf, ncs], bank(5), AF.Exp, scale=-1.0)
                p.ev("vector", "tensor_copy", srstag[0:2, dbuf, ncs],
                     psum[0:2, 6 * 512:6 * 512 + 512],
                     wait=_wl(g3, dydec.get(tt - 2) if nch == 0 else None))
                pe_ea, pe_ev = p.now("ea"), p.now("ev")
            eadec[tt] = p.now("ea")
            evsr[tt] = p.now("ev")
            mmcs[tt] = p.now("mm")
            dsp_d[tt] = p.dma("sync", f"y_d{dbuf}", sp_d[tt * P:(tt + 1) * P, :],
                              dstage[:, dbuf, :], wait=_wl(eadec[tt]))
            dsp_b[tt] = p.dma("sync", f"y_b{dbuf}", sp_b[tt * P:(tt + 1) * P, :],
                              bstage[:, dbuf, :])
            dsp_a[tt] = p.dma("sync", f"y_a{dbuf}", sp_ap[tt * P:(tt + 1) * P, :],
                              apstage[:, dbuf, :])
            dydec[tt] = p.dma("sync", f"y_sr{dbuf}", sp_sr[2 * tt:2 * tt + 2, :],
                              srstag[0:2, dbuf, :], wait=_wl(evsr[tt]))
        if upto < 4:
            alldma = [p.now(s) for s in list(p.cnt) if s[:2] in ("y_", "g_", "c_")]
            _tail(p, out_t, x_in, wait=_wl(p.now("ev"), p.now("ea"), p.now("mm"),
                                           *alldma))
            p.finalize(ctx.enter_context(nc.Block()))
            return nc
        freef("xfw", "tdw1", "tdw2", "u1", "e_tile", "dstage", "bstage",
                 "apstage", "srstag", "tdrep")

        srows = sb("srows", [32, C], F32)
        alt = sb("alt", [32, C], F32)
        eot = sb("eot", [32, C], BF16)
        d = p.dma("sync", "y_srl", srows, sp_sr,
                  wait=_wl(dydec[TT - 1], dydec[TT - 2]))
        for nch in range(4):
            ncs = slice(nch * 512, (nch + 1) * 512)
            g = p.mm(psum[0:32, 7 * 512:7 * 512 + 512], l32x, srows[0:32, ncs],
                     wait=_wl(d if nch == 0 else None,
                              p.now("ea") if nch > 0 else None))
            p.act(eot[0:32, ncs], psum[0:32, 7 * 512:7 * 512 + 512], AF.Exp,
                  scale=-1.0, wait=_wl(g))
            p.act(alt[0:32, ncs], srows[0:32, ncs], AF.Exp, scale=-1.0)
        ea_al = p.now("ea")
        evtr = {}
        for ic in range(KC):
            g1 = p.trans(psum[0:P, 0:32], alt[0:32, ic * P:(ic + 1) * P],
                         identf[0:32, 0:32],
                         wait=_wl(ea_al if ic == 0 else None, evtr.get(ic - 1)))
            p.ev("vector", "tensor_copy", alfm[:, ic, :], psum[0:P, 0:32], wait=_wl(g1))
            g2 = p.trans(psum_bf[0:P, 1024:1024 + 32], eot[0:32, ic * P:(ic + 1) * P],
                         identb[0:32, 0:32])
            evtr[ic] = p.ev("vector", "tensor_copy", eofm[:, ic, :],
                            psum_bf[0:P, 1024:1024 + 32], wait=_wl(g2))
        ev_alfm = p.now("ev")
        freef("srows", "alt", "eot")

        xft = sb("xft", [P, 2, KC, P], BF16)
        wfull = sb("wfull", [P, KC, C], BF16)
        sctile = sb("sctile", [P, 2, C], BF16)
        pstg = sb("pstg", [P, 4, 512], BF16)

        def proj_sweep(fname, W_in, spill, scale_sp, spill_wait=None, scale_wait=()):
            f = FPROJ[fname]
            start_mm, start_ev, start_ea = p.now("mm"), p.now("ev"), p.now("ea")
            dW = p.dma("gpsimd", "g_wf", wfull,
                       W_in.rearrange("(kc p) n -> p kc n", p=P),
                       wait=_wl(start_mm, start_ev, start_ea))
            done = {}
            dyst = {g: p.now(f"y_pj{g % 4}") for g in range(-4, 0)}
            mmt = {-2: start_mm, -1: start_mm}
            gidx = 0
            for tt in range(TT):
                tb = tt % 2
                dx = p.dma("gpsimd", f"g_xt{tb}", xft[:, tb, :, :],
                           sp_xf[f, :, tt * P:(tt + 1) * P]
                               .rearrange("(kc p) t -> p kc t", p=P),
                           wait=_wl(mmt[tt - 2],
                                    *(spill_wait if tt == 0 and spill_wait else [])))
                dsc = None
                if scale_sp is not None:
                    dsc = p.dma("sync", f"y_sc{tb}", sctile[:, tb, :],
                                scale_sp[tt * P:(tt + 1) * P, :],
                                wait=_wl(done.get(gidx - 5),
                                         start_ev if tt < 2 else None,
                                         *(scale_wait if tt == 0 else ())))
                for n in range(4):
                    ncs = slice(n * 512, (n + 1) * 512)
                    bk = gidx % 4
                    for kc in range(KC):
                        p.mm(bank(bk), xft[:, tb, kc, :], wfull[:, kc, ncs],
                             start=(kc == 0), stop=(kc == KC - 1),
                             wait=_wl(dW if gidx == 0 else None,
                                      start_ev if gidx == 0 else None,
                                      start_ea if gidx == 0 else None,
                                      dx if kc == 0 and n == 0 else None,
                                      done.get(gidx - 4) if kc == 0 else None)
                             if kc == 0 else ())
                    g = p.now("mm")
                    stg = gidx % 4
                    if scale_sp is not None:
                        done[gidx] = p.ev("vector", "tensor_mul", pstg[:, stg, :],
                                          bank(bk), sctile[:, tb, ncs],
                                          wait=_wl(g, dsc if n == 0 else None,
                                                   dyst.get(gidx - 4)))
                    elif fname == "g":
                        done[gidx] = p.act(pstg[:, stg, :], bank(bk), AF.Silu,
                                           wait=_wl(g, dyst.get(gidx - 4)))
                    else:
                        done[gidx] = p.act(pstg[:, stg, :], bank(bk), AF.Identity,
                                           wait=_wl(g, dyst.get(gidx - 4)))
                    dyst[gidx] = p.dma("sync", f"y_pj{stg}",
                                       spill[tt * P:(tt + 1) * P, ncs],
                                       pstg[:, stg, :], wait=_wl(done[gidx]))
                    gidx += 1
                mmt[tt] = p.now("mm")
            return [p.now(f"y_pj{s}") for s in range(4)]

        dy_k = proj_sweep("k", proj_w_in["k"], sp_kt, sp_b, spill_wait=dy_xf,
                          scale_wait=[dsp_b[TT - 1], dsp_b[TT - 2]])
        dy_r = proj_sweep("r", proj_w_in["r"], sp_rt, sp_ap,
                          scale_wait=[dsp_a[TT - 1], dsp_a[TT - 2]])

        dy_v = proj_sweep("v", proj_w_in["v"], sp_v, None)
        dy_g = proj_sweep("g", proj_w_in["g"], sp_g, None)
        urep = sb("urep", [P, C], F32)
        qst = sb("qst", [P, 3, 2, C], BF16)
        qtmp = sb("qtmp", [P, C], F32)
        c_urep = p.dma("scalar", "c_urp", urep, urep_in)
        evq = {}
        for tt in range(TT):
            tb = tt % 2
            d1 = p.dma("sync", f"y_q{tb}", qst[:, 0, tb, :],
                       sp_rt[tt * P:(tt + 1) * P, :],
                       wait=_wl(*(dy_r if tt == 0 else []), evq.get(tt - 2),
                                dsp_d[TT - 1] if tt == 0 else None,
                                dsp_d[TT - 2] if tt == 0 else None))
            p.dma("sync", f"y_q{tb}", qst[:, 1, tb, :],
                  sp_kt[tt * P:(tt + 1) * P, :])
            d3 = p.dma("sync", f"y_q{tb}", qst[:, 2, tb, :],
                       sp_d[tt * P:(tt + 1) * P, :])
            p.ev("vector", "tensor_mul", qtmp[:, :], qst[:, 0, tb, :],
                 qst[:, 1, tb, :],
                 wait=_wl(d3, c_urep if tt == 0 else None))
            p.ev("vector", "tensor_mul", qtmp[:, :], qtmp[:, :], qst[:, 2, tb, :])
            p.ev("vector", "tensor_mul", qtmp[:, :], qtmp[:, :], urep[:, :])
            evq[tt] = p.ev("vector", "tensor_reduce", q_all[:, tt, :],
                           qtmp.rearrange("p (h n) -> p h n", h=H), AX.X, ALU.add)
        ev_q = p.now("ev")
        if debug:
            for tt in range(TT):
                p.dma("sync", f"y_qs{tt % 2}", sp_q[tt * P:(tt + 1) * P, :],
                      q_all[:, tt, :], wait=_wl(ev_q, ("y_qs%d" % (tt % 2),
                                                       p.cnt.get("y_qs%d" % (tt % 2), 0))))
        freef("qst", "qtmp", "urep")

        freef("xft", "wfull", "sctile", "pstg")

        if upto < 5:
            alldma = [p.now(s) for s in list(p.cnt) if s[:2] in ("y_", "g_", "c_")]
            _tail(p, out_t, x_in, wait=_wl(p.now("ev"), p.now("ea"), p.now("mm"),
                                           *alldma))
            p.finalize(ctx.enter_context(nc.Block()))
            return nc

        y_full = sb("y_full", [P, KC, T], F32)
        s_st = sb("s_st", [P, KC, N], F32)
        s2b = sb("s2b", [P, KC, P], BF16)
        str_r = sb("str_r", [64, 2, C], BF16)
        str_k = sb("str_k", [64, 2, C], BF16)
        str_v = sb("str_v", [64, 2, C], BF16)
        rT_t = sb("rT_t", [P, KC, L], BF16)
        kT_t = sb("kT_t", [P, KC, L], BF16)
        mm_t = sb("mm_t", [64, KC, 2, L], BF16)

        p.ev("vector", "memset", s_st[:, :, :], 0.0)
        p.ev("vector", "memset", s2b[:, :, :], 0.0)
        ev_s0 = p.now("ev")

        bt = {}
        mm_chunk = {}
        s2b_all = {-1: ev_s0}
        bk_tok = {}
        for c in range(NCH):
            cb = c % 2
            bb = 4 * (c % 2)
            rows = slice(c * L, (c + 1) * L)
            d_r = p.dma("sync", f"y_cr{cb}", str_r[0:64, cb, :], sp_rt[rows, :],
                        wait=_wl(mm_chunk.get(c - 2),
                                 *(dy_r if c == 0 else []),
                                 *(dy_g if c == 0 else [])))
            d_k = p.dma("sync", f"y_ck{cb}", str_k[0:64, cb, :], sp_kt[rows, :])
            d_v = p.dma("sync", f"y_cv{cb}", str_v[0:64, cb, :], sp_v[rows, :])
            for ht in range(KC):
                p.trans(psum_bf[0:P, bb * 1024 + ht * L:bb * 1024 + (ht + 1) * L],
                        str_r[0:64, cb, ht * P:(ht + 1) * P], identb[0:64, 0:64],
                        wait=_wl(d_r if ht == 0 else None,
                                 d_k if ht == 0 else None,
                                 bk_tok.get(bb) if ht == 0 else None,
                                 bk_tok.get(bb + 1) if ht == 0 else None))
                p.trans(psum_bf[0:P, (bb + 1) * 1024 + ht * L:(bb + 1) * 1024 + (ht + 1) * L],
                        str_k[0:64, cb, ht * P:(ht + 1) * P], identb[0:64, 0:64])
            gtA = p.now("mm")
            eR = p.act(rT_t.rearrange("p a b -> p (a b)"),
                       psum_bf[0:P, bb * 1024:bb * 1024 + 1024],
                       AF.Identity, wait=_wl(gtA, mm_chunk.get(c - 1)))
            eK = p.act(kT_t.rearrange("p a b -> p (a b)"),
                       psum_bf[0:P, (bb + 1) * 1024:(bb + 1) * 1024 + 1024],
                       AF.Identity)
            eRK = eK
            for ht in range(KC):
                ba = bb + ht // 8
                col = (ht % 8) * L
                p.mm(psum[0:64, ba * 512 + col:ba * 512 + col + L],
                     kT_t[0:64, ht, :], rT_t[0:64, ht, :],
                     wait=_wl(eRK if ht == 0 else None))
                bc = bb + 2 + ht // 8
                p.mm(psum[0:64, bc * 512 + col:bc * 512 + col + L],
                     kT_t[64:128, ht, :], rT_t[64:128, ht, :],
                     wait=_wl(bk_tok.get(bb + 2) if ht == 0 else None,
                              bk_tok.get(bb + 3) if ht == 0 else None))
            gtB = p.now("mm")
            mtok = {}
            for i, (bs, coh, h0) in enumerate(((bb, 0, 0), (bb + 1, 0, 8),
                                               (bb + 2, 1, 0), (bb + 3, 1, 8))):
                mtok[bs] = p.ev(
                    "vector", "tensor_mul",
                    mm_t[0:64, h0:h0 + 8, coh, :],
                    psum[0:64, bs * 512:(bs + 1) * 512].rearrange(
                        "p (a b) -> p a b", a=8),
                    masksu[0:64, :].unsqueeze(1).broadcast_to([64, 8, L]),
                    wait=_wl(gtB if i == 0 else None))
            for ht in range(KC):
                ba = bb + ht // 8
                col = (ht % 8) * L
                jA = slice(ht * P, ht * P + 64)
                jB = slice(ht * P + 64, (ht + 1) * P)
                p.mm(psum[0:P, ba * 512 + col:ba * 512 + col + L],
                     s2b[:, ht, :], rT_t[:, ht, :], start=True, stop=False,
                     wait=_wl(mtok[bb] if ht == 0 else None,
                              mtok[bb + 1] if ht == 8 else None,
                              s2b_all[c - 1] if ht == 0 else None))
                p.mm(psum[0:64, ba * 512 + col:ba * 512 + col + L],
                     str_v[0:64, cb, jA], mm_t[0:64, ht, 0, :],
                     start=False, stop=True, tile_position=(0, 0),
                     wait=_wl(d_v if ht == 0 else None,
                              mtok[bb + 2] if ht == 0 else None,
                              mtok[bb + 3] if ht == 0 else None))
                p.mm(psum[64:128, ba * 512 + col:ba * 512 + col + L],
                     str_v[0:64, cb, jB], mm_t[0:64, ht, 1, :],
                     start=False, stop=True, tile_position=(0, 64))
            gtC = p.now("mm")
            ye1 = p.act(y_full[:, 0:8, rows], 
                        psum[0:P, bb * 512:(bb + 1) * 512].rearrange(
                            "p (a b) -> p a b", a=8),
                        AF.Identity, wait=_wl(gtC))
            ye2 = p.act(y_full[:, 8:16, rows],
                        psum[0:P, (bb + 1) * 512:(bb + 2) * 512].rearrange(
                            "p (a b) -> p a b", a=8),
                        AF.Identity)
            bk_tok[bb] = ye1
            bk_tok[bb + 1] = ye2
            for ht in range(KC):
                bd = bb + 2 + ht // 8
                col = (ht % 8) * L
                jA = slice(ht * P, ht * P + 64)
                jB = slice(ht * P + 64, (ht + 1) * P)
                iA = slice(ht * P, ht * P + 64)
                p.mm(psum[0:64, bd * 512 + col:bd * 512 + col + L],
                     str_k[0:64, cb, iA], str_v[0:64, cb, jA],
                     tile_position=(0, 0),
                     wait=_wl(mtok[bb + 2] if ht == 0 else None,
                              mtok[bb + 3] if ht == 0 else None))
                p.mm(psum[64:128, bd * 512 + col:bd * 512 + col + L],
                     str_k[0:64, cb, jB], str_v[0:64, cb, jB],
                     tile_position=(0, 64))
            gtD = p.now("mm")
            stok = {}
            for bs, h0 in ((bb + 2, 0), (bb + 3, 8)):
                p.ev("vector", "tensor_add", s_st[:, h0:h0 + 8, :],
                     s_st[:, h0:h0 + 8, :],
                     psum[0:P, bs * 512:(bs + 1) * 512].rearrange(
                         "p (a b) -> p a b", a=8),
                     wait=_wl(gtD if h0 == 0 else None))
                stok[bs] = p.ev(
                    "vector", "tensor_mul", s_st[:, h0:h0 + 8, :],
                    s_st[:, h0:h0 + 8, :],
                    alfm[:, h0:h0 + 8, c:c + 1].broadcast_to([P, 8, N]),
                    wait=_wl(ev_alfm if c == 0 and h0 == 0 else None))
            bk_tok[bb + 2] = stok[bb + 2]
            bk_tok[bb + 3] = stok[bb + 3]
            evS = p.now("ev")
            p.act(s2b[0:64, :, 0:64], s_st[0:64, :, :], AF.Identity,
                  wait=_wl(gtC, evS))
            p.act(s2b[64:128, :, 64:128], s_st[64:128, :, :], AF.Identity)
            mm_chunk[c] = p.now("mm")
            s2b_all[c] = p.now("ea")
        freef("str_r", "str_k", "str_v", "mm_t")

        smask_sb = sb("smask_sb", [P, 1], F32)
        flag_sb = sb("flag_sb", [1, 16], I32)
        fpoll_sb = sb("fpoll_sb", [1, 16], I32)
        par_sb = sb("par_sb", [1, 1], I32)
        sin = sb("sin", [P, KC, N], F32)
        sin2b = sb("sin2b", [P, KC, P], BF16)

        p.dma("gpsimd", "g_msk", smask_sb, smask_in)
        p.dma("gpsimd", "g_par", par_sb, parity_in)
        d_non = p.dma("gpsimd", "g_non", flag_sb[:, 0:1], nonce_in)
        ev_sfinal = p.now("ev")

        def exch_thunk(E, ev_h=p.semh["ev"], ev_v=ev_sfinal[1]):
            E.wait_ge(ev_h, ev_v)
            E.wait_ge(p.semh["g_non"], 16)
            E.wait_ge(p.semh["g_par"], 16)
            with (E.register("pr") as pr, E.register("offw") as offw,
                  E.register("offr") as offr, E.register("cnt") as cnt,
                  E.register("cc") as ccr, E.register("ff") as ff,
                  E.register("nv") as nv):
                E.reg_load(pr, par_sb[0:1, 0:1])
                E.reg_load(nv, flag_sb[0:1, 0:1])
                E.tensor_copy(flag_sb[:, :], flag_sb[:, 0:1].broadcast_to([1, 16])
                              ).then_inc(p.semh["fl"], 1)
                E.wait_ge(p.semh["fl"], 1)
                SZ = P * KC * N
                E.reg_mul(offw, pr, SZ)
                E.reg_mov(offr, SZ)
                E.reg_sub(offr, offr, offw)
                E.dma_start(out=bass.AP(sh_s, offw, [[KC * N, P], [1, KC * N]]),
                            in_=s_st[:, :, :]).then_inc(p.semh["g_sx"], 16)
                E.wait_ge(p.semh["g_sx"], 16)
                E.reg_mul(offw, pr, 16)
                E.reg_mov(offr, 16)
                E.reg_sub(offr, offr, offw)
                E.dma_start(out=bass.AP(sh_fl, offw, [[16, 1], [1, 16]]),
                            in_=flag_sb[:, :]).then_inc(p.semh["g_sx"], 16)
                E.wait_ge(p.semh["g_sx"], 32)
                E.reg_mov(cnt, 32)
                E.reg_mov(ccr, 1)
                with E.While(ccr):
                    E.reg_add(cnt, cnt, 16)
                    E.dma_start(out=fpoll_sb[:, :],
                                in_=bass.AP(sh_fl, offr, [[16, 1], [1, 16]])
                                ).then_inc(p.semh["g_sx"], 16)
                    E.wait_ge(p.semh["g_sx"], cnt)
                    E.reg_load(ff, fpoll_sb[0:1, 0:1])
                    E.reg_sub(ccr, ff, nv)
                E.reg_load(pr, par_sb[0:1, 0:1])
                E.reg_mul(offw, pr, SZ)
                E.reg_mov(offr, SZ)
                E.reg_sub(offr, offr, offw)
                E.dma_start(out=sin[:, :, :],
                            in_=bass.AP(sh_s, offr, [[KC * N, P], [1, KC * N]])
                            ).then_inc(p.semh["g_si"], 16)

        p.sem("g_sx", "g_si")
        if exch:
            p.q["gpsimd"].append(exch_thunk)
            p.cnt["g_si"] = 16
            g_si = ("g_si", 16)
        else:
            g_so = p.dma("gpsimd", "g_so", sp_sstub.ap(),
                         s_st[:, :, :].rearrange("p a b -> p (a b)"),
                         wait=_wl(ev_sfinal, ("g_non", 16), ("g_par", 16)))
            g_si = p.dma("gpsimd", "g_si", sin[:, :, :].rearrange("p a b -> p (a b)"),
                         sp_sstub.ap(), wait=_wl(g_so))

        evm = p.ev("vector", "tensor_scalar_mul", sin[:, :, :], sin[:, :, :],
                   smask_sb[:, 0:1], wait=_wl(g_si, ("g_msk", 16)))
        p.ev("vector", "memset", sin2b[:, :, :], 0.0)
        for ht in range(KC):
            p.ev("vector", "tensor_copy", sin2b[0:64, ht, 0:64], sin[0:64, ht, :])
            p.ev("vector", "tensor_copy", sin2b[64:128, ht, 64:128],
                 sin[64:128, ht, :])
        ev_sin = p.now("ev")

        str_c = sb("str_c", [P, 4, C], BF16)
        zt = sb("zt", [P, 2, 512], BF16)
        ea_scan = p.now("ea")
        evcg = {}
        for cg in range(4):
            gb = cg % 2
            d_c = p.dma("sync", "y_cc0", str_c[:, :, :],
                        sp_rt[cg * 512:(cg + 1) * 512, :]
                        .rearrange("(a p) c -> p a c", p=P),
                        wait=_wl(evcg.get(cg - 1), mm_chunk[NCH - 1]))
            for ht in range(KC):
                zb = 4 + (ht % 2)
                for a in range(4):
                    gtr = p.trans(psum_bf[0:P, zb * 1024 + a * P:zb * 1024 + (a + 1) * P],
                                  str_c[:, a, ht * P:(ht + 1) * P], identb,
                                  wait=_wl(d_c if a == 0 else None,
                                           bt.get(zb) if a == 0 else None))
                ze = p.ev("vector", "tensor_mul",
                          zt[:, gb, :].rearrange("p (a b) -> p a b", a=8),
                          psum_bf[0:P, zb * 1024:zb * 1024 + 512]
                          .rearrange("p (a b) -> p a b", a=8),
                          eofm[:, ht, cg * 8:(cg + 1) * 8].unsqueeze(2)
                          .broadcast_to([P, 8, L]),
                          wait=_wl(gtr, ev_sin if cg == 0 and ht == 0 else None,
                                   bt.get(100 + gb)))
                bt[zb] = ze
                cb2 = 6 + (ht % 2)
                gc = p.mm(bank(cb2), sin2b[:, ht, :], zt[:, gb, :],
                          wait=_wl(ze, bt.get(cb2)))
                ya = p.ev("vector", "tensor_add", y_full[:, ht, cg * 512:(cg + 1) * 512],
                          y_full[:, ht, cg * 512:(cg + 1) * 512], bank(cb2),
                          wait=_wl(gc, ea_scan if cg == 0 and ht == 0 else None))
                bt[cb2] = ya
                bt[100 + gb] = ya
            evcg[cg] = p.now("ev")
        freef("rT_t", "kT_t", "str_c", "zt", "s_st", "s2b", "sin", "sin2b",
              "smask_sb", "flag_sb", "fpoll_sb", "par_sb")

        ytok = sb("ytok", [P, C], F32)
        ysq = sb("ysq", [P, C], F32)
        vg_t = sb("vg_t", [P, 2, 2, C], BF16)
        gnst = sb("gnst", [P, 8, H], F32)
        lnw2 = sb("lnw2", [P, C], F32)
        lnb2 = sb("lnb2", [P, C], F32)
        yng_st = sb("yng_st", [P, 2, C], BF16)
        c_ln = p.dma("scalar", "c_ln2", lnw2, lnwrep_in)
        c_lnb = p.dma("scalar", "c_lnb", lnb2, lnbrep_in)
        dvg = {}
        evgn = {}
        dyng = {}
        gn_pg = {}
        for tt in range(TT):
            vb = tt % 2
            rows = slice(tt * P, (tt + 1) * P)
            d_v2 = p.dma("sync", f"y_gv{vb}", vg_t[:, vb, 0, :], sp_v[rows, :],
                         wait=_wl(evgn.get(tt - 2), gn_pg.get(tt - 2)))
            d_g2 = p.dma("sync", f"y_gg{vb}", vg_t[:, vb, 1, :], sp_g[rows, :])
            prev_tok = gn_pg.get(tt - 1)
            for ic in range(KC):
                tb2 = ic % 2
                gtr = p.trans(bank(tb2, cols=P), y_full[:, ic, rows],
                              identf,
                              wait=_wl(evcg[3] if tt == 0 and ic == 0 else None,
                                       bt.get(200 + tb2),
                                       prev_tok if ic == 0 else None))
                ce = p.ev("vector", "tensor_copy", ytok[:, ic * P:(ic + 1) * P],
                          bank(tb2, cols=P), wait=_wl(gtr))
                bt[200 + tb2] = ce
            p.ev("vector", "tensor_mul", ysq.rearrange("p (h n) -> p h n", h=H),
                 vg_t[:, vb, 0, :].rearrange("p (h n) -> p h n", h=H),
                 q_all[:, tt, :].unsqueeze(2).broadcast_to([P, H, N]),
                 wait=_wl(d_v2, ev_q if tt == 0 else None))
            ya2 = p.ev("vector", "tensor_add", ytok[:, :], ytok[:, :], ysq[:, :])
            p.ev("vector", "tensor_reduce", gnst[:, 0, :],
                 ytok.rearrange("p (h n) -> p h n", h=H), AX.X, ALU.add)
            sq = p.act(ysq[:, :], ytok[:, :], AF.Square, wait=_wl(ya2))
            p.ev("vector", "tensor_reduce", gnst[:, 1, :],
                 ysq.rearrange("p (h n) -> p h n", h=H), AX.X, ALU.add,
                 wait=_wl(sq))
            p.ev("vector", "tensor_scalar_mul", gnst[:, 2, :], gnst[:, 0, :],
                 1.0 / N)
            p.ev("vector", "tensor_scalar_mul", gnst[:, 3, :], gnst[:, 1, :],
                 1.0 / N)
            p.ev("vector", "tensor_mul", gnst[:, 4, :], gnst[:, 2, :],
                 gnst[:, 2, :])
            p.ev("vector", "tensor_sub", gnst[:, 5, :], gnst[:, 3, :],
                 gnst[:, 4, :])
            sqv = p.act(gnst[:, 6, :], gnst[:, 5, :], AF.Sqrt, bias=epsc[:, 0:1],
                        wait=_wl(p.now("ev")))
            p.ev("vector", "reciprocal", gnst[:, 7, :], gnst[:, 6, :],
                 wait=_wl(sqv))
            p.ev("vector", "tensor_sub", ytok.rearrange("p (h n) -> p h n", h=H),
                 ytok.rearrange("p (h n) -> p h n", h=H),
                 gnst[:, 2, :].unsqueeze(2).broadcast_to([P, H, N]))
            p.ev("vector", "tensor_mul", ytok.rearrange("p (h n) -> p h n", h=H),
                 ytok.rearrange("p (h n) -> p h n", h=H),
                 gnst[:, 7, :].unsqueeze(2).broadcast_to([P, H, N]))
            dvtok = p.now("ev")
            p.ev("gpsimd", "tensor_mul", ytok[:, :], ytok[:, :], lnw2[:, :],
                 wait=_wl(dvtok, c_ln if tt == 0 else None))
            p.ev("gpsimd", "tensor_add", ytok[:, :], ytok[:, :], lnb2[:, :],
                 wait=_wl(c_lnb if tt == 0 else None))
            ev_yng = p.ev("gpsimd", "tensor_mul", yng_st[:, vb, :], ytok[:, :],
                          vg_t[:, vb, 1, :],
                          wait=_wl(d_g2, dyng.get(tt - 2)))
            dyng[tt] = p.dma("sync", f"y_yo{vb}", sp_yng[rows, :],
                             yng_st[:, vb, :], wait=_wl(ev_yng))
            evgn[tt] = p.now("ev")
            gn_pg[tt] = p.now("pg")
        freef("ytok", "ysq", "vg_t", "gnst", "lnw2", "lnb2", "yng_st", "y_full",
              "alfm", "eofm", "q_all")

        if upto < 7:
            alldma = [p.now(s) for s in list(p.cnt) if s[:2] in ("y_", "g_", "c_")]
            _tail(p, out_t, x_in, wait=_wl(p.now("ev"), p.now("ea"), p.now("mm"),
                                           *alldma))
            p.finalize(ctx.enter_context(nc.Block()))
            return nc

        wo_sb = sb("wo_sb", [P, KC, C], BF16)
        yngt = sb("yngt", [P, 2, C], BF16)
        yngT = sb("yngT", [P, KC, P], BF16)
        xrt = sb("xrt", [P, 2, C], F32)
        resid = sb("resid", [P, C], F32)
        rsq = sb("rsq", [P, C], F32)
        lncol = sb("lncol", [P, 8], F32)
        xab = sb("xab", [P, C], BF16)
        xaT = sb("xaT", [P, KC, T], BF16)
        xa_st = sb("xa_st", [P, 2, C], F32)

        g_wo = p.dma("gpsimd", "g_wo", wo_sb,
                     wo_in.rearrange("(kc p) n -> p kc n", p=P),
                     wait=_wl(p.now("mm")))
        evwo = {}
        dxa = {}
        for tt in range(TT):
            vb = tt % 2
            rows = slice(tt * P, (tt + 1) * P)
            d_y = p.dma("sync", f"y_yi{vb}", yngt[:, vb, :], sp_yng[rows, :],
                        wait=_wl(dyng[tt], evwo.get(tt - 2)))
            d_x = p.dma("sync", f"y_xr{vb}", xrt[:, vb, :], x_in[rows, :])
            ttr = {}
            for ic in range(KC):
                tb2 = 4 + ic % 2
                gtr = p.trans(bank(tb2, dt=BF16, cols=P), yngt[:, vb, ic * P:(ic + 1) * P],
                              identb,
                              wait=_wl(d_y if ic == 0 else None, bt.get(300 + tb2)))
                ttr[ic] = p.act(yngT[:, ic, :], bank(tb2, dt=BF16, cols=P),
                                AF.Identity, wait=_wl(gtr))
                bt[300 + tb2] = ttr[ic]
            for n in range(4):
                ncs = slice(n * 512, (n + 1) * 512)
                ob = n % 4
                for jc in range(KC):
                    p.mm(bank(ob), yngT[:, jc, :], wo_sb[:, jc, ncs],
                         start=(jc == 0), stop=(jc == KC - 1),
                         wait=_wl(ttr[KC - 1] if jc == 0 else None,
                                  g_wo if tt == 0 and n == 0 and jc == 0 else None,
                                  bt.get(400 + ob) if jc == 0 else None))
                gw = p.now("mm")
                re = p.ev("vector", "tensor_add", resid[:, ncs], bank(ob),
                          xrt[:, vb, ncs], wait=_wl(gw, d_x if n == 0 else None))
                bt[400 + ob] = re
            rlast = p.ev("vector", "tensor_reduce", lncol[:, 0:1], resid[:, :],
                         AX.X, ALU.add)
            sq = p.act(rsq[:, :], resid[:, :], AF.Square, wait=_wl(rlast))
            p.ev("vector", "tensor_reduce", lncol[:, 1:2], rsq[:, :], AX.X,
                 ALU.add, wait=_wl(sq))
            p.ev("vector", "tensor_scalar_mul", lncol[:, 2:3], lncol[:, 0:1],
                 1.0 / C)
            p.ev("vector", "tensor_scalar_mul", lncol[:, 3:4], lncol[:, 1:2],
                 1.0 / C)
            p.ev("vector", "tensor_mul", lncol[:, 4:5], lncol[:, 2:3],
                 lncol[:, 2:3])
            p.ev("vector", "tensor_sub", lncol[:, 5:6], lncol[:, 3:4],
                 lncol[:, 4:5])
            sqv = p.act(lncol[:, 6:7], lncol[:, 5:6], AF.Sqrt, bias=epsc[:, 1:2],
                        wait=_wl(p.now("ev")))
            p.ev("vector", "reciprocal", lncol[:, 7:8], lncol[:, 6:7],
                 wait=_wl(sqv))
            exa = p.ev("vector", "tensor_scalar", xa_st[:, vb, :], resid[:, :],
                       lncol[:, 2:3], lncol[:, 7:8], ALU.subtract, ALU.mult,
                       wait=_wl(dxa.get(tt - 2)))
            dxa[tt] = p.dma("sync", f"y_xa{vb}", sp_xa[rows, :], xa_st[:, vb, :],
                            wait=_wl(exa))
            cb3 = p.ev("vector", "tensor_copy", xab[:, :], xa_st[:, vb, :])
            for ic in range(KC):
                tb2 = 6 + ic % 2
                gtr = p.trans(bank(tb2, dt=BF16, cols=P), xab[:, ic * P:(ic + 1) * P],
                              identb, wait=_wl(cb3 if ic == 0 else None,
                                               bt.get(500 + tb2)))
                ab = p.act(xaT[:, ic, rows], bank(tb2, dt=BF16, cols=P),
                           AF.Identity, wait=_wl(gtr))
                bt[500 + tb2] = ab
            evwo[tt] = p.now("ev")
        eawo = p.now("ea")
        freef("wo_sb", "yngt", "yngT", "xrt", "resid", "rsq", "lncol", "xab",
              "xa_st")

        if upto < 8:
            alldma = [p.now(s) for s in list(p.cnt) if s[:2] in ("y_", "g_", "c_")]
            _tail(p, out_t, x_in, wait=_wl(p.now("ev"), p.now("ea"), p.now("mm"),
                                           *alldma))
            p.finalize(ctx.enter_context(nc.Block()))
            return nc

        gate_sb = sb("gate_sb", [P, KC, NEXP], BF16)
        eb2 = sb("eb2", [P, NEXP], F32)
        sco = sb("sco", [P, 5, NEXP], F32)
        m12 = sb("m12", [P, 4], F32)
        gselb = sb("gselb", [P, NEXP], BF16)
        grows = sb("grows", [8, T], BF16)
        xaf = sb("xaf", [P, 2, C], F32)
        xlo = sb("xlo", [P, C], BF16)
        xloT = sb("xloT", [P, KC, P], BF16)

        c_gt = p.dma("gpsimd", "g_gt", gate_sb,
                     gate_in.rearrange("(kc p) e -> p kc e", p=P))
        c_eb = p.dma("scalar", "c_eb", eb2, ebrep_in)
        gtok = None
        ltok = None
        for tt in range(TT):
            tcols = slice(tt * P, (tt + 1) * P)
            fb7 = tt % 2
            dxa7 = p.dma("sync", f"y_x7{fb7}", xaf[:, fb7, :],
                         sp_xa[tt * P:(tt + 1) * P, :],
                         wait=_wl(dxa[tt] if tt < 2 else None, ltok))
            lc = p.ev("vector", "tensor_copy", xlo[:, :], xaf[:, fb7, :],
                      wait=_wl(dxa7))
            ltok = p.ev("vector", "tensor_sub", xlo[:, :], xaf[:, fb7, :],
                        xlo[:, :])
            ttr7 = None
            for ic in range(KC):
                tb7 = 6 + ic % 2
                gtr = p.trans(bank(tb7, dt=BF16, cols=P),
                              xlo[:, ic * P:(ic + 1) * P], identb,
                              wait=_wl(ltok if ic == 0 else None,
                                       bt.get(700 + tb7)))
                bt[700 + tb7] = ttr7 = p.ev(
                    "vector", "tensor_copy", xloT[:, ic, :],
                    bank(tb7, dt=BF16, cols=P), wait=_wl(gtr))
            for kc in range(KC):
                p.mm(bank(0, cols=NEXP), xaT[:, kc, tcols], gate_sb[:, kc, :],
                     start=(kc == 0), stop=False,
                     wait=_wl(c_gt if tt == 0 and kc == 0 else None,
                              gtok if kc == 0 else None))
            for kc in range(KC):
                p.mm(bank(0, cols=NEXP), xloT[:, kc, :], gate_sb[:, kc, :],
                     start=False, stop=(kc == KC - 1),
                     wait=_wl(ttr7 if kc == 0 else None))
            gmm = p.now("mm")
            p.ev("vector", "reduce_max", m12[:, 0:1], bank(0, cols=NEXP), AX.X,
                 wait=_wl(gmm))
            gtok = p.ev("vector", "tensor_scalar", sco[:, 0, :], bank(0, cols=NEXP),
                        m12[:, 0:1], None, ALU.subtract)
            exx = p.act(sco[:, 0, :], sco[:, 0, :], AF.Exp, wait=_wl(gtok))
            p.ev("vector", "tensor_reduce", m12[:, 1:2], sco[:, 0, :], AX.X,
                 ALU.add, wait=_wl(exx))
            p.ev("vector", "reciprocal", m12[:, 2:3], m12[:, 1:2])
            p.ev("vector", "tensor_scalar_mul", sco[:, 1, :], sco[:, 0, :],
                 m12[:, 2:3])
            p.ev("vector", "tensor_add", sco[:, 2, :], sco[:, 1, :], eb2[:, :],
                 wait=_wl(c_eb if tt == 0 else None))
            p.ev("vector", "reduce_max", m12[:, 0:1], sco[:, 2, :], AX.X)
            p.ev("vector", "tensor_scalar", sco[:, 3, :], sco[:, 2, :],
                 m12[:, 0:1], None, ALU.is_equal)
            p.ev("vector", "scalar_tensor_tensor", sco[:, 4, :], sco[:, 3, :],
                 1.0e30, sco[:, 2, :], ALU.mult, ALU.subtract)
            p.ev("vector", "tensor_reduce", m12[:, 1:2], sco[:, 4, :], AX.X,
                 ALU.min)
            p.ev("vector", "tensor_scalar_mul", m12[:, 2:3], m12[:, 1:2], -1.0)
            p.ev("vector", "tensor_scalar", sco[:, 4, :], sco[:, 2, :],
                 m12[:, 2:3], None, ALU.is_equal)
            p.ev("vector", "tensor_add", sco[:, 3, :], sco[:, 3, :], sco[:, 4, :])
            evg = p.ev("vector", "tensor_mul", gselb[:, :], sco[:, 1, :],
                       sco[:, 3, :])
            gt2 = p.trans(psum_bf[0:8, 1024:1024 + P], gselb[:, :], identb,
                          wait=_wl(evg, bt.get(600)))
            bt[600] = p.act(grows[0:8, tcols], psum_bf[0:8, 1024:1024 + P],
                            AF.Identity, wait=_wl(gt2))
        d_gr = p.dma("sync", "y_gw", sp_gr.ap(), grows[0:8, :], wait=_wl(bt[600]))

        xe = sb("xe", [P, KC, T], BF16)
        grep_t = sb("grep_t", [P, T], BF16)
        w1sl = sb("w1sl", [P, 2, 2, KC, P], BF16)
        silst = sb("silst", [P, 2, 512], F32)
        hstg = sb("hstg", [P, 4, 512], BF16)

        hs_tok = {}
        mm_hc = {}
        hmprev = {}
        p8_ev, p8_ea = p.now("ev"), p.now("ea")
        for E in range(9):
            first = (E == 0)
            if E < 8:
                dgr = p.dma("sync", f"y_ge{E % 2}", grep_t[:, :],
                            sp_gr.ap()[E:E + 1, :].to_broadcast([P, T]),
                            wait=_wl(d_gr if E < 2 else None,
                                     mm_hc.get(E - 2)))
                evx_kc = {}
                for kc in range(KC):
                    evx_kc[kc] = p.ev("vector", "tensor_mul", xe[:, kc, :],
                                      xaT[:, kc, :], grep_t[:, :],
                                      wait=_wl(dgr if kc == 0 else None,
                                               mm_hc.get(E - 1) if kc == 0 else None))
                xsrc = xe
            else:
                evx_kc = {}
                xsrc = xaT
            for hc in range(HC):
                hb = hc % 2
                hcols = slice(hc * P, (hc + 1) * P)
                wprev = (mm_hc.get((E, hc - 2)) if hc >= 2
                         else mm_hc.get((E - 1, HC - 2 + hc)))
                dW1 = p.dma("gpsimd", f"g_h1{hb}", w1sl[:, 0, hb, :, :],
                            fw1_in[E, :, hcols].rearrange("(kc p) h -> p kc h", p=P)
                            if E < 8 else
                            sw1_in[:, hcols].rearrange("(kc p) h -> p kc h", p=P),
                            wait=_wl(wprev))
                dW3 = p.dma("gpsimd", f"g_h3{hb}", w1sl[:, 1, hb, :, :],
                            fw3_in[E, :, hcols].rearrange("(kc p) h -> p kc h", p=P)
                            if E < 8 else
                            sw3_in[:, hcols].rearrange("(kc p) h -> p kc h", p=P))
                for tch in range(4):
                    tcs = slice(tch * 512, (tch + 1) * 512)
                    pa, pb = tch % 2, 2 + tch % 2
                    stg = tch % 4
                    w0 = _wl(dW1, dW3,
                             p8_ev if E == 0 and hc == 0 else None,
                             p8_ea if E == 0 and hc == 0 else None,
                             hs_tok.get((E, hc, tch - 2)) or
                             hs_tok.get((E, hc - 1, tch + 2)) or
                             hs_tok.get((E - 1, HC - 1, tch - 2)) or
                             (hs_tok.get((E - 1, HC - 1, tch + 2)) if tch < 2
                              else None))
                    for kc in range(KC):
                        wkc = list(w0 if kc == 0 else ())
                        if hc == 0 and tch == 0 and kc in evx_kc:
                            wkc.append(evx_kc[kc])
                        p.mm(bank(pa), w1sl[:, 0, hb, kc, :], xsrc[:, kc, tcs],
                             start=(kc == 0), stop=(kc == KC - 1), wait=wkc)
                        p.mm(bank(pb), w1sl[:, 1, hb, kc, :], xsrc[:, kc, tcs],
                             start=(kc == 0), stop=(kc == KC - 1))
                    ghm = p.now("mm")
                    gidx2 = (E * HC + hc) * 4 + tch
                    sa = p.act(silst[:, gidx2 % 2, :], bank(pa), AF.Silu,
                               wait=_wl(ghm, hmprev.get(gidx2 - 2)))
                    hm = p.ev("vector", "tensor_mul", hstg[:, stg, :],
                              silst[:, gidx2 % 2, :], bank(pb),
                              wait=_wl(sa, hs_tok.get((E, hc, tch - 4)) or
                                       hs_tok.get((E, hc - 1, tch)) or
                                       hs_tok.get((E - 1, HC - 1, tch))))
                    hmprev[gidx2] = hm
                    hs_tok[(E, hc, tch)] = p.dma(
                        "sync", f"y_hs{stg}",
                        sp_h[E, hc * P:(hc + 1) * P, tcs], hstg[:, stg, :],
                        wait=_wl(hm))
                mm_hc[(E, hc)] = p.now("mm")
            mm_hc[E] = p.now("mm")
        freef("xe", "grep_t", "w1sl", "silst", "hstg", "gate_sb", "eb2", "sco",
              "m12", "gselb", "grows", "xaT", "xaf", "xlo", "xloT")

        if upto < 9:
            alldma = [p.now(s) for s in list(p.cnt) if s[:2] in ("y_", "g_", "c_")]
            _tail(p, out_t, x_in, wait=_wl(p.now("ev"), p.now("ea"), p.now("mm"),
                                           *alldma))
            p.finalize(ctx.enter_context(nc.Block()))
            return nc

        HHC = HC // 2
        acc = sb("acc", [P, TT, C], F32)
        w2h = sb("w2h", [P, 2, HHC, C], BF16)
        hti2 = sb("hti2", [P, 2, HHC, P], BF16)

        acc_tok = {}
        mm_eh = {}
        mm_tt9 = {}
        for e in range(9):
            for hh in range(2):
                g2 = e * 2 + hh
                wb = g2 % 2
                hrows = slice(hh * 896, (hh + 1) * 896)
                srcw = (fw2_in[e, hrows, :] if e < 8 else sw2_in[hrows, :])
                dw2 = p.dma("gpsimd", f"g_wh{wb}", w2h[:, wb, :, :],
                            srcw.rearrange("(hc p) n -> p hc n", p=P),
                            wait=_wl(mm_eh.get(g2 - 2)))
                for tt in range(TT):
                    hb = tt % 2
                    trows = slice(tt * P, (tt + 1) * P)
                    wde = []
                    if hh == 0 and tt == 0:
                        wde = [hs_tok.get((e, HC - 1, s)) for s in range(4)]
                    dh = p.dma("sync", f"y_h2{hb}", hti2[:, hb, :, :],
                               sp_h[e, hrows, trows]
                               .rearrange("(hc p) t -> p hc t", p=P),
                               wait=_wl(mm_tt9.get((g2, tt - 2)) or
                                        mm_tt9.get((g2 - 1, TT - 2 + tt)),
                                        *wde))
                    for cq in range(4):
                        ccols = slice(cq * 512, (cq + 1) * 512)
                        ob = (tt * 4 + cq) % 4
                        for hc in range(HHC):
                            p.mm(bank(ob), hti2[:, hb, hc, :],
                                 w2h[:, wb, hc, ccols],
                                 start=(hc == 0), stop=(hc == HHC - 1),
                                 wait=_wl(dh if cq == 0 else None,
                                          dw2 if tt == 0 else None,
                                          acc_tok.get((g2, tt * 4 + cq - 4)) or
                                          acc_tok.get((g2 - 1, 60 + tt * 4 + cq)))
                                 if hc == 0 else ())
                        gmm9 = p.now("mm")
                        if g2 == 0:
                            acc_tok[(g2, tt * 4 + cq)] = p.ev(
                                "vector", "tensor_copy", acc[:, tt, ccols],
                                bank(ob), wait=_wl(gmm9))
                        else:
                            acc_tok[(g2, tt * 4 + cq)] = p.ev(
                                "vector", "tensor_add", acc[:, tt, ccols],
                                acc[:, tt, ccols], bank(ob), wait=_wl(gmm9))
                    mm_tt9[(g2, tt)] = p.now("mm")
                mm_eh[g2] = p.now("mm")
        freef("w2h", "hti2")

        xat2 = sb("xat2", [P, 2, C], F32)
        res2 = sb("res2", [P, C], F32)
        r2sq = sb("r2sq", [P, C], F32)
        lnc2 = sb("lnc2", [P, 8], F32)
        ostf = sb("ostf", [P, 2, C], F32)

        sttok = {}
        evfin = {}
        for tt in range(TT):
            fb = tt % 2
            trows = slice(tt * P, (tt + 1) * P)
            d_xa2 = p.dma("sync", f"y_fx{fb}", xat2[:, fb, :], sp_xa[trows, :],
                          wait=_wl(dxa[tt] if tt < 2 else None,
                                   evfin.get(tt - 2)))
            ra = p.ev("vector", "tensor_add", res2[:, :], acc[:, tt, :],
                      xat2[:, fb, :],
                      wait=_wl(d_xa2, acc_tok[(17, tt * 4 + 3)]))
            rl = p.ev("vector", "tensor_reduce", lnc2[:, 0:1], res2[:, :], AX.X,
                      ALU.add)
            sq2 = p.act(r2sq[:, :], res2[:, :], AF.Square, wait=_wl(rl))
            p.ev("vector", "tensor_reduce", lnc2[:, 1:2], r2sq[:, :], AX.X,
                 ALU.add, wait=_wl(sq2))
            p.ev("vector", "tensor_scalar_mul", lnc2[:, 2:3], lnc2[:, 0:1],
                 1.0 / C)
            p.ev("vector", "tensor_scalar_mul", lnc2[:, 3:4], lnc2[:, 1:2],
                 1.0 / C)
            p.ev("vector", "tensor_mul", lnc2[:, 4:5], lnc2[:, 2:3],
                 lnc2[:, 2:3])
            p.ev("vector", "tensor_sub", lnc2[:, 5:6], lnc2[:, 3:4],
                 lnc2[:, 4:5])
            sqv2 = p.act(lnc2[:, 6:7], lnc2[:, 5:6], AF.Sqrt, bias=epsc[:, 1:2],
                         wait=_wl(p.now("ev"), ev_eps if tt == 0 else None))
            p.ev("vector", "reciprocal", lnc2[:, 7:8], lnc2[:, 6:7],
                 wait=_wl(sqv2))
            evfin[tt] = p.ev("vector", "tensor_scalar", ostf[:, fb, :],
                             res2[:, :], lnc2[:, 2:3], lnc2[:, 7:8],
                             ALU.subtract, ALU.mult,
                             wait=_wl(sttok.get(tt - 2)))
            sttok[tt] = p.dma("sync", f"st{fb}", out_t[trows, :], ostf[:, fb, :],
                              wait=_wl(evfin[tt]))

        alldma = [p.now(s) for s in list(p.cnt) if s[:2] in ("y_", "g_", "c_", "st")]
        p.emit("sync", lambda E: E.nop(),
               wait=_wl(p.now("ev"), p.now("ea"), p.now("mm"), p.now("pg"),
                        *alldma))
        p.finalize(ctx.enter_context(nc.Block()))
        return nc


def _tail(p, out_t, x_src, wait=()):
    st = p.dma("sync", "st", out_t, x_src, wait=wait)
    p.emit("sync", lambda E: E.nop(), wait=[st])


_NC_CACHE = {}


def _bd_tri_np(n, blocks, strict):
    b = np.triu(np.ones((n, n), np.float32), 1 if strict else 0)
    out = np.zeros((n * blocks, n * blocks), np.float32)
    for i in range(blocks):
        out[i * n:(i + 1) * n, i * n:(i + 1) * n] = b
    return out


def _consts():
    c = {}
    c["l2i"] = _bd_tri_np(64, 2, strict=False)
    c["l2x"] = _bd_tri_np(64, 2, strict=True)
    c["l32x"] = np.triu(np.ones((32, 32), np.float32), 1)
    msu = np.triu(np.ones((L, L), np.float32), 1)
    c["masksu"] = np.concatenate([msu, msu], 0)
    ch = np.zeros((P, 2), np.float32)
    ch[0:64, 0] = 1.0
    ch[64:128, 1] = 1.0
    c["chones"] = ch
    c["identf"] = np.eye(P, dtype=np.float32)
    c["identb"] = np.eye(P, dtype=np.float32)
    return c


def make_in_maps(inputs, nonce=None):
    g = lambda k: np.asarray(inputs[k], np.float32)
    x = g("L0_attn_x")
    consts = _consts()
    if nonce is None:
        nonce = int(np.random.randint(1, 2 ** 30))

    def col16(v):
        return np.ascontiguousarray(np.asarray(v, np.float32).reshape(-1).reshape(KC, P).T)

    maas = [col16(g("L0_attn_time_maa_" + s)) for s in "xwkvrg"]
    maatab = np.ascontiguousarray(np.stack(maas, 1))

    shared = {
        "tm_w1": g("L0_attn_w1"), "tm_w2": g("L0_attn_w2"),
        "td_w1": g("L0_attn_td_w1"), "td_w2": g("L0_attn_td_w2"),
        "wr": g("L0_attn_wr"), "wk": g("L0_attn_wk"), "wv": g("L0_attn_wv"),
        "wg": g("L0_attn_wg"), "wo": g("L0_attn_wo"),
        "gate_w": g("L0_ffn_gate_w"),
        "fw1": g("L0_ffn_w1"), "fw2": g("L0_ffn_w2"), "fw3": g("L0_ffn_w3"),
        "sw1": g("L0_ffn_sw1"), "sw2": g("L0_ffn_sw2"), "sw3": g("L0_ffn_sw3"),
        "maatab": maatab,
        "urep": np.ascontiguousarray(
            np.broadcast_to(g("L0_attn_time_faaaa").reshape(-1), (P, C))),
        "lnwrep": np.ascontiguousarray(
            np.broadcast_to(g("L0_attn_ln_w").reshape(-1), (P, C))),
        "lnbrep": np.ascontiguousarray(
            np.broadcast_to(g("L0_attn_ln_b").reshape(-1), (P, C))),
        "tdrep": np.ascontiguousarray(
            np.broadcast_to(g("L0_attn_time_decay").reshape(-1), (P, C))),
        "ebrep": np.ascontiguousarray(
            np.broadcast_to(g("L0_ffn_expert_bias").reshape(-1), (P, NEXP))),
        **consts,
    }
    in_maps = []
    for core in range(8):
        b, j = core // 2, core % 2
        sl = np.ascontiguousarray(x[b, j * T:(j + 1) * T])
        xp = np.zeros(C, np.float32) if j == 0 else np.asarray(x[b, j * T - 1], np.float32)
        m = dict(shared)
        m["x"] = sl
        m["xprev"] = col16(xp)
        m["parity"] = np.array([[j]], np.int32)
        m["smask"] = np.full((128, 1), float(j), np.float32)
        m["nonce"] = np.array([[nonce]], np.int32)
        in_maps.append(m)
    return in_maps


def kernel(**inputs):
    if "full" not in _NC_CACHE:
        _NC_CACHE["full"] = build_program()
    nc = _NC_CACHE["full"]
    in_maps = make_in_maps(inputs)
    res = run_bass_kernel_spmd(nc, in_maps, core_ids=list(range(8)))
    outs = [res.results[i]["out"] for i in range(8)]
    full = np.stack(outs, 0).reshape(4, 2, T, C).reshape(4, 4096, C)
    return np.ascontiguousarray(full.astype(np.float32))


# version 05 (final)
# speedup vs baseline: 1.2535x; 1.2535x over previous
import numpy as np
import ml_dtypes
from contextlib import ExitStack

import concourse.bass as bass
import concourse.mybir as mybir
from concourse.bass_utils import run_bass_kernel_spmd

F32 = mybir.dt.float32
BF16 = mybir.dt.bfloat16
I32 = mybir.dt.int32
U8 = mybir.dt.uint8
AF = mybir.ActivationFunctionType
ALU = mybir.AluOpType
AX = mybir.AxisListType

P = 128
T = 2048
C = 2048
H, N = 32, 64
L = 64
NCH = T // L
KC = C // P
TT = T // P
HID = 1792
HC = HID // P
NEXP = 8
DMIX, DDEC = 32, 64
GN_EPS = 1e-5 * 64.0
LN_EPS = 1e-5

ENGINES = ("sync", "tensor", "vector", "scalar", "gpsimd")
SB_BYTES = 200 * 1024
PS_F32 = 4096


class Mem:
    def __init__(self, total=SB_BYTES):
        self.free_list = [(0, total)]
        self.regions = {}

    def alloc(self, name, nbytes, align=64):
        assert name not in self.regions, name
        if not hasattr(self, "log"):
            self.log = {}
        nbytes = -(-nbytes // align) * align
        for i, (off, sz) in enumerate(self.free_list):
            aoff = -(-off // align) * align
            pad = aoff - off
            if sz >= nbytes + pad:
                rest = []
                if pad:
                    rest.append((off, pad))
                if sz - pad - nbytes:
                    rest.append((aoff + nbytes, sz - pad - nbytes))
                self.free_list[i:i + 1] = rest
                self.regions[name] = (aoff, nbytes)
                self.log[name] = (aoff, nbytes)
                return aoff
        raise MemoryError(f"SBUF arena full for {name} ({nbytes}B): {self.regions}")

    def free(self, *names):
        for name in names:
            off, sz = self.regions.pop(name)
            self.free_list.append((off, sz))
        self.free_list.sort()
        merged = []
        for off, sz in self.free_list:
            if merged and merged[-1][0] + merged[-1][1] == off:
                merged[-1] = (merged[-1][0], merged[-1][1] + sz)
            else:
                merged.append((off, sz))
        self.free_list = merged


class Prog:

    def __init__(self, nc, ctx):
        self.nc = nc
        self.ctx = ctx
        self.q = {e: [] for e in ENGINES}
        self.semh = {}
        self.cnt = {}

    def sem(self, *names):
        for name in names:
            if name in self.semh:
                continue
            self.semh[name] = self.ctx.enter_context(self.nc.semaphore(name))
            self.cnt[name] = 0

    def emit(self, eng, fn, wait=(), inc=None, n=1):
        wl = []
        for s, v in wait:
            v = int(v)
            assert v <= self.cnt[s], (s, v, self.cnt[s])
            if v > 0:
                wl.append((self.semh[s], v))
        ih = self.semh[inc] if inc else None

        def thunk(E, wl=wl, fn=fn, ih=ih, n=n):
            for h, v in wl:
                E.wait_ge(h, v)
            ins = fn(E)
            if ih is not None:
                ins.then_inc(ih, n)
        self.q[eng].append(thunk)
        if inc:
            self.cnt[inc] += n
            return (inc, self.cnt[inc])

    def dma(self, eng, sem, out, in_, wait=(), **kw):
        self.sem(sem)
        return self.emit(eng, lambda E, o=out, i=in_, kw=kw: E.dma_start(out=o, in_=i, **kw),
                         wait, sem, 16)

    def mm(self, out, lhsT, rhs, start=True, stop=True, wait=(), tile_position=None):
        return self.emit(
            "tensor",
            lambda E, o=out, l=lhsT, r=rhs, s=start, p=stop, tp=tile_position:
                E.matmul(o, l, r, start=s, stop=p, tile_position=tp),
            wait, "mm", 1)

    def trans(self, out, in_, ident, wait=()):
        return self.emit("tensor",
                         lambda E, o=out, i=in_, d=ident: E.transpose(o, i, d),
                         wait, "mm", 1)

    def act(self, out, in_, func, bias=0.0, scale=1.0, accum_out=None, wait=()):
        wait = list(wait) + [("ea", self.cnt["ea"])]
        return self.emit(
            "scalar",
            lambda E, o=out, i=in_, f=func, b=bias, s=scale, a=accum_out:
                E.activation(o, i, f, bias=b, scale=s, accum_out=a),
            wait, "ea", 1)

    def ev(self, eng, name, *args, wait=(), **kw):
        inc = {"vector": "ev", "scalar": "ea", "gpsimd": "pg", "tensor": "mm"}[eng]
        if eng != "tensor":
            wait = list(wait) + [(inc, self.cnt[inc])]
        return self.emit(eng,
                         lambda E, nm=name, a=args, k=kw: getattr(E, nm)(*a, **k),
                         wait, inc, 1)

    def now(self, s):
        self.sem(s)
        return (s, self.cnt[s])

    def fence(self, engs=ENGINES, sems=None):
        if sems is None:
            sems = list(self.cnt)
        targets = [(self.semh[s], self.cnt[s]) for s in sems if self.cnt[s] > 0]
        for e in engs:
            def thunk(E, wl=list(targets)):
                for h, v in wl:
                    E.wait_ge(h, v)
            self.q[e].append(thunk)

    def finalize(self, block):
        for eng in ENGINES:
            thunks = self.q[eng]
            if not thunks:
                continue
            dec = getattr(block, eng)

            def body(E, thunks=thunks):
                for t in thunks:
                    t(E)
            dec(body)


def _wl(*tokens):
    return [t for t in tokens if t is not None]


_LAST_MEM = [None]


def build_program(debug=False, upto=99, races=True, exch=True):
    nc = bass.Bass("TRN2", target_bir_lowering=False,
                   detect_race_conditions=races)

    def din(name, shape, dtype=F32):
        return nc.dram_tensor(name, list(shape), dtype, kind="ExternalInput").ap()

    x_in = din("x", [T, C])
    xprev_in = din("xprev", [P, KC])
    w1_in = din("tm_w1", [C, 160])
    w2_in = din("tm_w2", [5, DMIX, C])
    tdw1_in = din("td_w1", [C, DDEC])
    tdw2_in = din("td_w2", [DDEC, C])
    proj_w_in = {"k": din("wk", [C, C]), "v": din("wv", [C, C]),
                 "r": din("wr", [C, C]), "g": din("wg", [C, C])}
    wo_in = din("wo", [C, C])
    gate_in = din("gate_w", [C, NEXP])
    fw1_in = din("fw1", [NEXP, C, HID])
    fw2_in = din("fw2", [NEXP, HID, C])
    fw3_in = din("fw3", [NEXP, C, HID])
    sw1_in = din("sw1", [C, HID])
    sw2_in = din("sw2", [HID, C])
    sw3_in = din("sw3", [C, HID])

    maatab_in = din("maatab", [P, 6, KC])
    urep_in = din("urep", [P, C])
    lnwrep_in = din("lnwrep", [P, C])
    lnbrep_in = din("lnbrep", [P, C])
    tdrep_in = din("tdrep", [P, C])
    ebrep_in = din("ebrep", [P, NEXP])
    l2i_in = din("l2i", [P, P])
    l2x_in = din("l2x", [P, P])
    l32x_in = din("l32x", [32, 32])
    masksu_in = din("masksu", [P, L])
    chones_in = din("chones", [P, 2])
    identf_in = din("identf", [P, P])
    identb_in = din("identb", [P, P])
    parity_in = din("parity", [1, 1], I32)
    smask_in = din("smask", [P, 1])
    nonce_in = din("nonce", [1, 1], I32)

    out_t = nc.dram_tensor("out", [T, C], F32, kind="ExternalOutput").ap()

    spk = "ExternalOutput" if debug else "Internal"

    def scratch(name, shape, dtype):
        return nc.dram_tensor(name, list(shape), dtype, kind=spk).ap()

    sp_xbf = scratch("sp_xbf", [T, C], BF16)
    sp_xf = scratch("sp_xf", [5, C, T], BF16)
    sp_rt = scratch("sp_rt", [T, C], BF16)
    sp_kt = scratch("sp_kt", [T, C], BF16)
    sp_v = scratch("sp_v", [T, C], BF16)
    sp_g = scratch("sp_g", [T, C], BF16)
    sp_b = scratch("sp_b", [T, C], BF16)
    sp_ap = scratch("sp_ap", [T, C], BF16)
    sp_d = scratch("sp_d", [T, C], BF16)
    sp_sr = scratch("sp_sr", [NCH, C], F32)
    sp_q = scratch("sp_q", [T, H], F32)
    sp_yng = scratch("sp_yng", [T, C], BF16)
    sp_gr = nc.dram_tensor("sp_gr", [8, T], BF16, kind=spk)
    sp_h = nc.dram_tensor("sp_h", [9, HID, T], BF16, kind="Internal").ap()
    sp_xa = scratch("sp_xa", [T, C], F32)
    sp_out2 = scratch("sp_out2", [T, C], F32)
    sh_s = nc.dram_tensor("sh_s", [2, P, KC * N], F32, addr_space="Shared")
    sp_sstub = nc.dram_tensor("sp_sstub", [P, KC * N], F32)
    sh_fl = nc.dram_tensor("sh_fl", [2, 16], I32, addr_space="Shared")

    FPROJ = {"w": 0, "k": 1, "v": 2, "r": 3, "g": 4}

    mem = Mem()
    _LAST_MEM[0] = mem
    with ExitStack() as ctx:
        arena = ctx.enter_context(nc.sbuf_tensor("arena", [P, SB_BYTES], U8)).ap()
        psum = ctx.enter_context(nc.psum_tensor("psA", [P, PS_F32], F32)).ap()
        psum_bf = psum.bitcast(BF16)

        def bank(i, rows=P, cols=512, dt=F32):
            if dt is F32:
                return psum[0:rows, i * 512:i * 512 + cols]
            return psum_bf[0:rows, i * 1024:i * 1024 + cols]

        def sb(name, shape, dtype, align=64):
            nb = int(np.prod(shape[1:])) * mybir.dt.size(dtype)
            off = mem.alloc(name, nb, align)
            flat = arena[0:shape[0], off:off + nb].bitcast(dtype)
            if len(shape) > 2:
                names = " ".join(f"d{i}" for i in range(1, len(shape)))
                flat = flat.rearrange(f"p ({names}) -> p {names}",
                                      **{f"d{i}": int(shape[i]) for i in range(1, len(shape))})
            return flat

        p = Prog(nc, ctx)
        p.sem("mm", "ev", "ea", "pg", "fl")

        def freef(*names):
            mem.free(*names)
            p.fence()

        maatab = sb("maatab", [P, 6, KC], F32)
        l2i = sb("l2i", [P, P], F32)
        l2x = sb("l2x", [P, P], F32)
        l32x = sb("l32x", [32, 32], F32)
        masksu = sb("masksu", [P, L], F32)
        chones = sb("chones", [P, 2], F32)
        identf = sb("identf", [P, P], F32)
        identb = sb("identb", [P, P], BF16)
        epsc = sb("epsc", [P, 2], F32)
        alfm = sb("alfm", [P, KC, NCH], F32)
        eofm = sb("eofm", [P, KC, NCH], BF16)
        q_all = sb("q_all", [P, TT, H], F32)

        for dst, src in ((maatab, maatab_in),
                         (l2i, l2i_in), (l2x, l2x_in), (l32x, l32x_in),
                         (masksu, masksu_in), (chones, chones_in),
                         (identf, identf_in)):
            p.dma("scalar", "c_con", dst, src)
        d_consts_c = p.now("c_con")
        d_consts_g = p.dma("gpsimd", "g_idb", identb, identb_in)
        p.ev("vector", "memset", epsc[:, 0:1], float(GN_EPS))
        ev_eps = p.ev("vector", "memset", epsc[:, 1:2], float(LN_EPS))

        x_fm = sb("x_fm", [P, KC, T], BF16)
        xx_fm = sb("xx_fm", [P, KC, T], BF16)
        xtk = sb("xtk", [P, 2, C], BF16)
        xprev_c = sb("xprev_c", [P, KC], BF16)

        d_xbf = p.dma("gpsimd", "g_xbf", sp_xbf, x_in)
        g_xp = p.dma("gpsimd", "g_xpv", xprev_c, xprev_in)

        mm_tile = {}
        evdone = {}
        for tt in range(TT):
            buf = tt % 2
            d = p.dma("sync", f"y_xtk{buf}", xtk[:, buf, :],
                      sp_xbf[tt * P:(tt + 1) * P, :],
                      wait=_wl(d_xbf, mm_tile.get(tt - 2)))
            for ic in range(KC):
                bk = ic % 2
                idx = tt * KC + ic
                m = p.trans(bank(bk, dt=BF16, cols=P), xtk[:, buf, ic * P:(ic + 1) * P],
                            identb,
                            wait=_wl(d if ic == 0 else None,
                                     d_consts_g if idx == 0 else None,
                                     evdone.get(idx - 2)))
                evdone[idx] = p.ev("vector", "tensor_copy",
                                   x_fm[:, ic, tt * P:(tt + 1) * P],
                                   bank(bk, dt=BF16, cols=P), wait=_wl(m))
            mm_tile[tt] = p.now("mm")
        ev_xload = p.now("ev")

        for ic in range(KC):
            p.ev("vector", "tensor_sub", xx_fm[:, ic, 1:T], x_fm[:, ic, 0:T - 1],
                 x_fm[:, ic, 1:T], wait=_wl(ev_xload if ic == 0 else None))
            p.ev("vector", "tensor_sub", xx_fm[:, ic, 0:1], xprev_c[:, ic:ic + 1],
                 x_fm[:, ic, 0:1], wait=_wl(g_xp if ic == 0 else None))
        ev_xx = p.now("ev")
        freef("xtk", "xprev_c")

        w1sb = sb("w1sb", [P, KC, 160], BF16)
        w2sb = sb("w2sb", [32, 5, C], BF16)
        g_w1 = p.dma("gpsimd", "g_w1s", w1sb, w1_in.rearrange("(kc p) m -> p kc m", p=P))
        g_w2 = p.dma("gpsimd", "g_w2s", w2sb, w2_in.rearrange("f d c -> d f c"))
        m5 = sb("m5", [32, 5, T], BF16)
        xxp = sb("xxp", [P, KC, 512], BF16)

        mm_mgrp = {}
        ea_mgrp = {}
        for tch in range(4):
            tc = slice(tch * 512, (tch + 1) * 512)
            for kc in range(KC):
                p.ev("vector", "scalar_tensor_tensor",
                     xxp[:, kc, :], xx_fm[:, kc, tc], maatab[:, 0, kc:kc + 1],
                     x_fm[:, kc, tc], ALU.mult, ALU.add,
                     wait=_wl(ev_xx if tch == 0 and kc == 0 else None,
                              mm_mgrp.get(tch - 1) if kc == 0 else None))
            evp = p.now("ev")
            bkA, bkB = 2 * (tch % 2), 2 * (tch % 2) + 1
            for kc in range(KC):
                p.mm(bank(bkA, rows=96), w1sb[:, kc, 0:96], xxp[:, kc, :],
                     start=(kc == 0), stop=(kc == KC - 1),
                     wait=_wl(evp if kc == 0 else None,
                              g_w1 if tch == 0 and kc == 0 else None,
                              ea_mgrp.get(tch - 2) if kc == 0 else None))
                p.mm(bank(bkB, rows=64), w1sb[:, kc, 96:160], xxp[:, kc, :],
                     start=(kc == 0), stop=(kc == KC - 1))
            mm_mgrp[tch] = p.now("mm")
            for f in range(5):
                if f < 3:
                    src = psum[(f % 3) * 32:(f % 3) * 32 + 32, bkA * 512:bkA * 512 + 512]
                else:
                    src = psum[(f - 3) * 32:(f - 3) * 32 + 32, bkB * 512:bkB * 512 + 512]
                p.act(m5[:, f, tc], src, AF.Tanh,
                      wait=_wl(mm_mgrp[tch] if f == 0 else None))
            ea_mgrp[tch] = p.now("ea")
        ea_m = p.now("ea")
        if upto < 2:
            alldma = [p.now(s) for s in list(p.cnt) if s[:2] in ("y_", "g_", "c_")]
            _tail(p, out_t, x_in, wait=_wl(p.now("ev"), p.now("ea"), p.now("mm"),
                                           *alldma))
            p.finalize(ctx.enter_context(nc.Block()))
            return nc
        freef("w1sb", "xxp")

        xfst = sb("xfst", [P, 8, 512], BF16)
        tmpf = sb("tmpf", [P, 4, 512], F32)
        pdone = {}
        pst = {}
        piece = 0
        for fname, f in sorted(FPROJ.items(), key=lambda kv: kv[1]):
            for kc in range(KC):
                for tch in range(4):
                    tc = slice(tch * 512, (tch + 1) * 512)
                    bk = 4 + piece % 4
                    mmx = p.mm(bank(bk), w2sb[:, f, kc * P:(kc + 1) * P],
                               m5[:, f, tc],
                               wait=_wl(ea_m if piece == 0 else None,
                                        g_w2 if piece == 0 else None,
                                        pdone.get(piece - 4)))
                    p.ev("vector", "scalar_tensor_tensor",
                         tmpf[:, piece % 4, :], bank(bk), maatab[:, 1 + f, kc:kc + 1],
                         xx_fm[:, kc, tc], ALU.add, ALU.mult,
                         wait=_wl(mmx, pdone.get(piece - 4)))
                    stt_t = p.now("ev")
                    pdone[piece] = p.ev("gpsimd", "tensor_add", xfst[:, piece % 8, :],
                                        tmpf[:, piece % 4, :], x_fm[:, kc, tc],
                                        wait=_wl(stt_t, pst.get(piece - 8)))
                    pst[piece] = p.dma(
                        "sync", f"y_xf{piece % 4}",
                        sp_xf[f, kc * P:(kc + 1) * P, tch * 512:(tch + 1) * 512],
                        xfst[:, piece % 8, :], wait=_wl(pdone[piece],
                                                        pst.get(piece - 4)))
                    piece += 1
        dy_xf = [p.now(f"y_xf{s}") for s in range(4)]
        freef("x_fm", "xx_fm", "m5", "w2sb", "xfst", "tmpf")

        if upto < 3:
            alldma = [p.now(s) for s in list(p.cnt) if s[:2] in ("y_", "g_", "c_")]
            _tail(p, out_t, x_in, wait=_wl(p.now("ev"), p.now("ea"), p.now("mm"),
                                           *alldma))
            p.finalize(ctx.enter_context(nc.Block()))
            return nc

        xfw = sb("xfw", [P, KC, T], BF16)
        tdw1 = sb("tdw1", [P, KC, DDEC], BF16)
        tdw2 = sb("tdw2", [64, C], BF16)
        tdrep = sb("tdrep", [P, C], F32)
        u1 = sb("u1", [64, T], BF16)
        e_tile = sb("e_tile", [P, 2, C], F32)
        dstage = sb("dstage", [P, 2, C], BF16)
        bstage = sb("bstage", [P, 2, C], BF16)
        apstage = sb("apstage", [P, 2, C], BF16)
        srstag = sb("srstag", [2, 2, C], F32)

        g_xfw = p.dma("gpsimd", "g_xfw", xfw,
                      sp_xf[0].rearrange("(kc p) t -> p kc t", p=P),
                      wait=_wl(*dy_xf))
        g_td1 = p.dma("gpsimd", "g_td1", tdw1,
                      tdw1_in.rearrange("(kc p) m -> p kc m", p=P))
        g_td2 = p.dma("gpsimd", "g_td2", tdw2, tdw2_in)
        c_tdr = p.dma("scalar", "c_tdr", tdrep, tdrep_in)

        ea_u1g = {}
        for tch in range(4):
            bk = tch % 2
            for kc in range(KC):
                p.mm(bank(bk, rows=64), tdw1[:, kc, :],
                     xfw[:, kc, tch * 512:(tch + 1) * 512],
                     start=(kc == 0), stop=(kc == KC - 1),
                     wait=_wl(g_td1 if tch == 0 and kc == 0 else None,
                              g_xfw if tch == 0 and kc == 0 else None,
                              ea_u1g.get(tch - 2) if kc == 0 else None))
            g = p.now("mm")
            ea_u1g[tch] = p.act(u1[0:64, tch * 512:(tch + 1) * 512], bank(bk, rows=64),
                                AF.Tanh, wait=_wl(g))
        ea_u1 = p.now("ea")

        edone = {}
        dsp_d, dsp_b, dsp_a = {}, {}, {}
        wgrp = {}
        eadec = {}
        evsr = {}
        dydec = {}
        mmcs = {}
        for tt in range(TT):
            dbuf = tt % 2
            for nch in range(4):
                ncs = slice(nch * 512, (nch + 1) * 512)
                bk = 2 + (nch % 2)
                wgrp[(tt, nch)] = p.mm(
                    bank(bk), u1[0:64, tt * P:(tt + 1) * P], tdw2[0:64, ncs],
                    wait=_wl(g_td2 if tt == 0 and nch == 0 else None,
                             ea_u1 if tt == 0 and nch == 0 else None,
                             edone.get((tt, nch - 2)) or edone.get((tt - 1, 2 + nch))))
                edone[(tt, nch)] = p.ev(
                    "vector", "tensor_add", e_tile[:, tt % 2, ncs], bank(bk),
                    tdrep[:, ncs],
                    wait=_wl(wgrp[(tt, nch)],
                             c_tdr if tt == 0 and nch == 0 else None,
                             mmcs.get(tt - 2) if nch == 0 else None,
                             dydec.get(tt - 2) if nch == 0 else None,
                             eadec.get(tt - 2) if nch == 0 else None))
                p.act(e_tile[:, tt % 2, ncs], e_tile[:, tt % 2, ncs], AF.Exp,
                      wait=_wl(edone[(tt, nch)]))
                p.act(dstage[:, dbuf, ncs], e_tile[:, tt % 2, ncs], AF.Exp,
                      scale=-1.0,
                      wait=_wl(dsp_d.get(tt - 2) if nch == 0 else None))
            ev_et = p.now("ea")
            pe_ea = pe_ev = None
            for nch in range(4):
                ncs = slice(nch * 512, (nch + 1) * 512)
                g1 = p.mm(bank(4), l2i, e_tile[:, tt % 2, ncs],
                          wait=_wl(ev_et if nch == 0 else None,
                                   eadec.get(tt - 1) if nch == 0 else None,
                                   evsr.get(tt - 1) if nch == 0 else None,
                                   pe_ea, pe_ev))
                g2 = p.mm(bank(5), l2x, e_tile[:, tt % 2, ncs])
                g3 = p.mm(psum[0:2, 6 * 512:6 * 512 + 512], chones,
                          e_tile[:, tt % 2, ncs])
                p.act(bstage[:, dbuf, ncs], bank(4), AF.Exp,
                      wait=_wl(g3, dsp_b.get(tt - 2) if nch == 0 else None,
                               dsp_a.get(tt - 2) if nch == 0 else None))
                p.act(apstage[:, dbuf, ncs], bank(5), AF.Exp, scale=-1.0)
                p.ev("vector", "tensor_copy", srstag[0:2, dbuf, ncs],
                     psum[0:2, 6 * 512:6 * 512 + 512],
                     wait=_wl(g3, dydec.get(tt - 2) if nch == 0 else None))
                pe_ea, pe_ev = p.now("ea"), p.now("ev")
            eadec[tt] = p.now("ea")
            evsr[tt] = p.now("ev")
            mmcs[tt] = p.now("mm")
            dsp_d[tt] = p.dma("sync", f"y_d{dbuf}", sp_d[tt * P:(tt + 1) * P, :],
                              dstage[:, dbuf, :], wait=_wl(eadec[tt]))
            dsp_b[tt] = p.dma("sync", f"y_b{dbuf}", sp_b[tt * P:(tt + 1) * P, :],
                              bstage[:, dbuf, :])
            dsp_a[tt] = p.dma("sync", f"y_a{dbuf}", sp_ap[tt * P:(tt + 1) * P, :],
                              apstage[:, dbuf, :])
            dydec[tt] = p.dma("sync", f"y_sr{dbuf}", sp_sr[2 * tt:2 * tt + 2, :],
                              srstag[0:2, dbuf, :], wait=_wl(evsr[tt]))
        if upto < 4:
            alldma = [p.now(s) for s in list(p.cnt) if s[:2] in ("y_", "g_", "c_")]
            _tail(p, out_t, x_in, wait=_wl(p.now("ev"), p.now("ea"), p.now("mm"),
                                           *alldma))
            p.finalize(ctx.enter_context(nc.Block()))
            return nc
        freef("xfw", "tdw1", "tdw2", "u1", "e_tile", "dstage", "bstage",
                 "apstage", "srstag", "tdrep")

        srows = sb("srows", [32, C], F32)
        alt = sb("alt", [32, C], F32)
        eot = sb("eot", [32, C], BF16)
        d = p.dma("sync", "y_srl", srows, sp_sr,
                  wait=_wl(dydec[TT - 1], dydec[TT - 2]))
        for nch in range(4):
            ncs = slice(nch * 512, (nch + 1) * 512)
            g = p.mm(psum[0:32, 7 * 512:7 * 512 + 512], l32x, srows[0:32, ncs],
                     wait=_wl(d if nch == 0 else None,
                              p.now("ea") if nch > 0 else None))
            p.act(eot[0:32, ncs], psum[0:32, 7 * 512:7 * 512 + 512], AF.Exp,
                  scale=-1.0, wait=_wl(g))
            p.act(alt[0:32, ncs], srows[0:32, ncs], AF.Exp, scale=-1.0)
        ea_al = p.now("ea")
        evtr = {}
        for ic in range(KC):
            g1 = p.trans(psum[0:P, 0:32], alt[0:32, ic * P:(ic + 1) * P],
                         identf[0:32, 0:32],
                         wait=_wl(ea_al if ic == 0 else None, evtr.get(ic - 1)))
            p.ev("vector", "tensor_copy", alfm[:, ic, :], psum[0:P, 0:32], wait=_wl(g1))
            g2 = p.trans(psum_bf[0:P, 1024:1024 + 32], eot[0:32, ic * P:(ic + 1) * P],
                         identb[0:32, 0:32])
            evtr[ic] = p.ev("vector", "tensor_copy", eofm[:, ic, :],
                            psum_bf[0:P, 1024:1024 + 32], wait=_wl(g2))
        ev_alfm = p.now("ev")
        freef("srows", "alt", "eot")

        xft = sb("xft", [P, 2, KC, 512], BF16)
        wfull = sb("wfull", [P, KC, C], BF16)
        sctile = sb("sctile", [P, 2, C], BF16)
        pstg = sb("pstg", [P, 4, 512], BF16)

        def proj_sweep(fname, W_in, spill, scale_sp, spill_wait=None, scale_wait=()):
            f = FPROJ[fname]
            start_mm, start_ev, start_ea = p.now("mm"), p.now("ev"), p.now("ea")
            dW = p.dma("gpsimd", "g_wf", wfull,
                       W_in.rearrange("(kc p) n -> p kc n", p=P),
                       wait=_wl(start_mm, start_ev, start_ea))
            done = {}
            dyst = {g: p.now(f"y_pj{g % 4}") for g in range(-4, 0)}
            mmt = {-2: start_mm, -1: start_mm}
            gidx = 0
            dx = None
            for tt in range(TT):
                tb = (tt // 4) % 2
                if tt % 4 == 0:
                    dx = p.dma("gpsimd", f"g_xt{tb}", xft[:, tb, :, :],
                               sp_xf[f, :, tt * P:(tt + 4) * P]
                                   .rearrange("(kc p) t -> p kc t", p=P),
                               wait=_wl(mmt[tt - 5] if tt >= 8 else start_mm,
                                        *(spill_wait if tt == 0 and spill_wait
                                          else [])))
                dsc = None
                if scale_sp is not None:
                    dsc = p.dma("sync", f"y_sc{tb}", sctile[:, tb, :],
                                scale_sp[tt * P:(tt + 1) * P, :],
                                wait=_wl(done.get(gidx - 5),
                                         start_ev if tt < 2 else None,
                                         *(scale_wait if tt == 0 else ())))
                for n in range(4):
                    ncs = slice(n * 512, (n + 1) * 512)
                    bk = gidx % 4
                    toff = (tt % 4) * P
                    for kc in range(KC):
                        p.mm(bank(bk), xft[:, tb, kc, toff:toff + P],
                             wfull[:, kc, ncs],
                             start=(kc == 0), stop=(kc == KC - 1),
                             wait=_wl(dW if gidx == 0 else None,
                                      start_ev if gidx == 0 else None,
                                      start_ea if gidx == 0 else None,
                                      dx if kc == 0 and n == 0 else None,
                                      done.get(gidx - 4) if kc == 0 else None)
                             if kc == 0 else ())
                    g = p.now("mm")
                    stg = gidx % 4
                    if scale_sp is not None:
                        done[gidx] = p.ev("vector", "tensor_mul", pstg[:, stg, :],
                                          bank(bk), sctile[:, tb, ncs],
                                          wait=_wl(g, dsc if n == 0 else None,
                                                   dyst.get(gidx - 4)))
                    elif fname == "g":
                        done[gidx] = p.act(pstg[:, stg, :], bank(bk), AF.Silu,
                                           wait=_wl(g, dyst.get(gidx - 4)))
                    else:
                        done[gidx] = p.act(pstg[:, stg, :], bank(bk), AF.Identity,
                                           wait=_wl(g, dyst.get(gidx - 4)))
                    dyst[gidx] = p.dma("sync", f"y_pj{stg}",
                                       spill[tt * P:(tt + 1) * P, ncs],
                                       pstg[:, stg, :], wait=_wl(done[gidx]))
                    gidx += 1
                mmt[tt] = p.now("mm")
            return [p.now(f"y_pj{s}") for s in range(4)]

        dy_k = proj_sweep("k", proj_w_in["k"], sp_kt, sp_b, spill_wait=dy_xf,
                          scale_wait=[dsp_b[TT - 1], dsp_b[TT - 2]])
        dy_r = proj_sweep("r", proj_w_in["r"], sp_rt, sp_ap,
                          scale_wait=[dsp_a[TT - 1], dsp_a[TT - 2]])

        dy_v = proj_sweep("v", proj_w_in["v"], sp_v, None)
        dy_g = proj_sweep("g", proj_w_in["g"], sp_g, None)
        urep = sb("urep", [P, C], F32)
        qst = sb("qst", [P, 3, 2, C], BF16)
        qtmp = sb("qtmp", [P, C], F32)
        c_urep = p.dma("scalar", "c_urp", urep, urep_in)
        evq = {}
        for tt in range(TT):
            tb = tt % 2
            d1 = p.dma("sync", f"y_q{tb}", qst[:, 0, tb, :],
                       sp_rt[tt * P:(tt + 1) * P, :],
                       wait=_wl(*(dy_r if tt == 0 else []), evq.get(tt - 2),
                                dsp_d[TT - 1] if tt == 0 else None,
                                dsp_d[TT - 2] if tt == 0 else None))
            p.dma("sync", f"y_q{tb}", qst[:, 1, tb, :],
                  sp_kt[tt * P:(tt + 1) * P, :])
            d3 = p.dma("sync", f"y_q{tb}", qst[:, 2, tb, :],
                       sp_d[tt * P:(tt + 1) * P, :])
            p.ev("vector", "tensor_mul", qtmp[:, :], qst[:, 0, tb, :],
                 qst[:, 1, tb, :],
                 wait=_wl(d3, c_urep if tt == 0 else None))
            p.ev("vector", "tensor_mul", qtmp[:, :], qtmp[:, :], qst[:, 2, tb, :])
            p.ev("vector", "tensor_mul", qtmp[:, :], qtmp[:, :], urep[:, :])
            evq[tt] = p.ev("vector", "tensor_reduce", q_all[:, tt, :],
                           qtmp.rearrange("p (h n) -> p h n", h=H), AX.X, ALU.add)
        ev_q = p.now("ev")
        if debug:
            for tt in range(TT):
                p.dma("sync", f"y_qs{tt % 2}", sp_q[tt * P:(tt + 1) * P, :],
                      q_all[:, tt, :], wait=_wl(ev_q, ("y_qs%d" % (tt % 2),
                                                       p.cnt.get("y_qs%d" % (tt % 2), 0))))
        freef("qst", "qtmp", "urep")

        freef("xft", "wfull", "sctile", "pstg")

        if upto < 5:
            alldma = [p.now(s) for s in list(p.cnt) if s[:2] in ("y_", "g_", "c_")]
            _tail(p, out_t, x_in, wait=_wl(p.now("ev"), p.now("ea"), p.now("mm"),
                                           *alldma))
            p.finalize(ctx.enter_context(nc.Block()))
            return nc

        y_full = sb("y_full", [P, KC, T], F32)
        s_st = sb("s_st", [P, KC, N], F32)
        s2b = sb("s2b", [P, KC, P], BF16)
        str_r = sb("str_r", [64, 2, C], BF16)
        str_k = sb("str_k", [64, 2, C], BF16)
        str_v = sb("str_v", [64, 2, C], BF16)
        rT_t = sb("rT_t", [P, KC, L], BF16)
        kT_t = sb("kT_t", [P, KC, L], BF16)
        mm_t = sb("mm_t", [64, KC, 2, L], BF16)

        p.ev("vector", "memset", s_st[:, :, :], 0.0)
        p.ev("vector", "memset", s2b[:, :, :], 0.0)
        ev_s0 = p.now("ev")

        bt = {}
        mm_chunk = {}
        s2b_all = {-1: ev_s0}
        bk_tok = {}
        for c in range(NCH):
            cb = c % 2
            bb = 4 * (c % 2)
            rows = slice(c * L, (c + 1) * L)
            d_r = p.dma("sync", f"y_cr{cb}", str_r[0:64, cb, :], sp_rt[rows, :],
                        wait=_wl(mm_chunk.get(c - 2),
                                 *(dy_r if c == 0 else []),
                                 *(dy_g if c == 0 else [])))
            d_k = p.dma("sync", f"y_ck{cb}", str_k[0:64, cb, :], sp_kt[rows, :])
            d_v = p.dma("sync", f"y_cv{cb}", str_v[0:64, cb, :], sp_v[rows, :])
            for ht in range(KC):
                p.trans(psum_bf[0:P, bb * 1024 + ht * L:bb * 1024 + (ht + 1) * L],
                        str_r[0:64, cb, ht * P:(ht + 1) * P], identb[0:64, 0:64],
                        wait=_wl(d_r if ht == 0 else None,
                                 d_k if ht == 0 else None,
                                 bk_tok.get(bb) if ht == 0 else None,
                                 bk_tok.get(bb + 1) if ht == 0 else None))
                p.trans(psum_bf[0:P, (bb + 1) * 1024 + ht * L:(bb + 1) * 1024 + (ht + 1) * L],
                        str_k[0:64, cb, ht * P:(ht + 1) * P], identb[0:64, 0:64])
            gtA = p.now("mm")
            eR = p.act(rT_t.rearrange("p a b -> p (a b)"),
                       psum_bf[0:P, bb * 1024:bb * 1024 + 1024],
                       AF.Identity, wait=_wl(gtA, mm_chunk.get(c - 1)))
            eK = p.act(kT_t.rearrange("p a b -> p (a b)"),
                       psum_bf[0:P, (bb + 1) * 1024:(bb + 1) * 1024 + 1024],
                       AF.Identity)
            eRK = eK
            for ht in range(KC):
                ba = bb + ht // 8
                col = (ht % 8) * L
                p.mm(psum[0:64, ba * 512 + col:ba * 512 + col + L],
                     kT_t[0:64, ht, :], rT_t[0:64, ht, :],
                     wait=_wl(eRK if ht == 0 else None))
                bc = bb + 2 + ht // 8
                p.mm(psum[0:64, bc * 512 + col:bc * 512 + col + L],
                     kT_t[64:128, ht, :], rT_t[64:128, ht, :],
                     wait=_wl(bk_tok.get(bb + 2) if ht == 0 else None,
                              bk_tok.get(bb + 3) if ht == 0 else None))
            gtB = p.now("mm")
            mtok = {}
            for i, (bs, coh, h0) in enumerate(((bb, 0, 0), (bb + 1, 0, 8),
                                               (bb + 2, 1, 0), (bb + 3, 1, 8))):
                mtok[bs] = p.ev(
                    "vector", "tensor_mul",
                    mm_t[0:64, h0:h0 + 8, coh, :],
                    psum[0:64, bs * 512:(bs + 1) * 512].rearrange(
                        "p (a b) -> p a b", a=8),
                    masksu[0:64, :].unsqueeze(1).broadcast_to([64, 8, L]),
                    wait=_wl(gtB if i == 0 else None))
            for ht in range(KC):
                ba = bb + ht // 8
                col = (ht % 8) * L
                jA = slice(ht * P, ht * P + 64)
                jB = slice(ht * P + 64, (ht + 1) * P)
                p.mm(psum[0:P, ba * 512 + col:ba * 512 + col + L],
                     s2b[:, ht, :], rT_t[:, ht, :], start=True, stop=False,
                     wait=_wl(mtok[bb] if ht == 0 else None,
                              mtok[bb + 1] if ht == 8 else None,
                              s2b_all[c - 1] if ht == 0 else None))
                p.mm(psum[0:64, ba * 512 + col:ba * 512 + col + L],
                     str_v[0:64, cb, jA], mm_t[0:64, ht, 0, :],
                     start=False, stop=True, tile_position=(0, 0),
                     wait=_wl(d_v if ht == 0 else None,
                              mtok[bb + 2] if ht == 0 else None,
                              mtok[bb + 3] if ht == 0 else None))
                p.mm(psum[64:128, ba * 512 + col:ba * 512 + col + L],
                     str_v[0:64, cb, jB], mm_t[0:64, ht, 1, :],
                     start=False, stop=True, tile_position=(0, 64))
            gtC = p.now("mm")
            ye1 = p.act(y_full[:, 0:8, rows], 
                        psum[0:P, bb * 512:(bb + 1) * 512].rearrange(
                            "p (a b) -> p a b", a=8),
                        AF.Identity, wait=_wl(gtC))
            ye2 = p.act(y_full[:, 8:16, rows],
                        psum[0:P, (bb + 1) * 512:(bb + 2) * 512].rearrange(
                            "p (a b) -> p a b", a=8),
                        AF.Identity)
            bk_tok[bb] = ye1
            bk_tok[bb + 1] = ye2
            for ht in range(KC):
                bd = bb + 2 + ht // 8
                col = (ht % 8) * L
                jA = slice(ht * P, ht * P + 64)
                jB = slice(ht * P + 64, (ht + 1) * P)
                iA = slice(ht * P, ht * P + 64)
                p.mm(psum[0:64, bd * 512 + col:bd * 512 + col + L],
                     str_k[0:64, cb, iA], str_v[0:64, cb, jA],
                     tile_position=(0, 0),
                     wait=_wl(mtok[bb + 2] if ht == 0 else None,
                              mtok[bb + 3] if ht == 0 else None))
                p.mm(psum[64:128, bd * 512 + col:bd * 512 + col + L],
                     str_k[0:64, cb, jB], str_v[0:64, cb, jB],
                     tile_position=(0, 64))
            gtD = p.now("mm")
            stok = {}
            for bs, h0 in ((bb + 2, 0), (bb + 3, 8)):
                p.ev("vector", "tensor_add", s_st[:, h0:h0 + 8, :],
                     s_st[:, h0:h0 + 8, :],
                     psum[0:P, bs * 512:(bs + 1) * 512].rearrange(
                         "p (a b) -> p a b", a=8),
                     wait=_wl(gtD if h0 == 0 else None))
                stok[bs] = p.ev(
                    "vector", "tensor_mul", s_st[:, h0:h0 + 8, :],
                    s_st[:, h0:h0 + 8, :],
                    alfm[:, h0:h0 + 8, c:c + 1].broadcast_to([P, 8, N]),
                    wait=_wl(ev_alfm if c == 0 and h0 == 0 else None))
            bk_tok[bb + 2] = stok[bb + 2]
            bk_tok[bb + 3] = stok[bb + 3]
            evS = p.now("ev")
            p.act(s2b[0:64, :, 0:64], s_st[0:64, :, :], AF.Identity,
                  wait=_wl(gtC, evS))
            p.act(s2b[64:128, :, 64:128], s_st[64:128, :, :], AF.Identity)
            mm_chunk[c] = p.now("mm")
            s2b_all[c] = p.now("ea")
        freef("str_r", "str_k", "str_v", "mm_t")

        smask_sb = sb("smask_sb", [P, 1], F32)
        flag_sb = sb("flag_sb", [1, 16], I32)
        fpoll_sb = sb("fpoll_sb", [1, 16], I32)
        par_sb = sb("par_sb", [1, 1], I32)
        sin = sb("sin", [P, KC, N], F32)
        sin2b = sb("sin2b", [P, KC, P], BF16)

        p.dma("gpsimd", "g_msk", smask_sb, smask_in)
        p.dma("gpsimd", "g_par", par_sb, parity_in)
        d_non = p.dma("gpsimd", "g_non", flag_sb[:, 0:1], nonce_in)
        ev_sfinal = p.now("ev")

        def exch_thunk(E, ev_h=p.semh["ev"], ev_v=ev_sfinal[1]):
            E.wait_ge(ev_h, ev_v)
            E.wait_ge(p.semh["g_non"], 16)
            E.wait_ge(p.semh["g_par"], 16)
            with (E.register("pr") as pr, E.register("offw") as offw,
                  E.register("offr") as offr, E.register("cnt") as cnt,
                  E.register("cc") as ccr, E.register("ff") as ff,
                  E.register("nv") as nv):
                E.reg_load(pr, par_sb[0:1, 0:1])
                E.reg_load(nv, flag_sb[0:1, 0:1])
                E.tensor_copy(flag_sb[:, :], flag_sb[:, 0:1].broadcast_to([1, 16])
                              ).then_inc(p.semh["fl"], 1)
                E.wait_ge(p.semh["fl"], 1)
                SZ = P * KC * N
                E.reg_mul(offw, pr, SZ)
                E.reg_mov(offr, SZ)
                E.reg_sub(offr, offr, offw)
                E.dma_start(out=bass.AP(sh_s, offw, [[KC * N, P], [1, KC * N]]),
                            in_=s_st[:, :, :]).then_inc(p.semh["g_sx"], 16)
                E.wait_ge(p.semh["g_sx"], 16)
                E.reg_mul(offw, pr, 16)
                E.reg_mov(offr, 16)
                E.reg_sub(offr, offr, offw)
                E.dma_start(out=bass.AP(sh_fl, offw, [[16, 1], [1, 16]]),
                            in_=flag_sb[:, :]).then_inc(p.semh["g_sx"], 16)
                E.wait_ge(p.semh["g_sx"], 32)
                E.reg_mov(cnt, 32)
                E.reg_mov(ccr, 1)
                with E.While(ccr):
                    E.reg_add(cnt, cnt, 16)
                    E.dma_start(out=fpoll_sb[:, :],
                                in_=bass.AP(sh_fl, offr, [[16, 1], [1, 16]])
                                ).then_inc(p.semh["g_sx"], 16)
                    E.wait_ge(p.semh["g_sx"], cnt)
                    E.reg_load(ff, fpoll_sb[0:1, 0:1])
                    E.reg_sub(ccr, ff, nv)
                E.reg_load(pr, par_sb[0:1, 0:1])
                E.reg_mul(offw, pr, SZ)
                E.reg_mov(offr, SZ)
                E.reg_sub(offr, offr, offw)
                E.dma_start(out=sin[:, :, :],
                            in_=bass.AP(sh_s, offr, [[KC * N, P], [1, KC * N]])
                            ).then_inc(p.semh["g_si"], 16)

        p.sem("g_sx", "g_si")
        if exch:
            p.q["gpsimd"].append(exch_thunk)
            p.cnt["g_si"] = 16
            g_si = ("g_si", 16)
        else:
            g_so = p.dma("gpsimd", "g_so", sp_sstub.ap(),
                         s_st[:, :, :].rearrange("p a b -> p (a b)"),
                         wait=_wl(ev_sfinal, ("g_non", 16), ("g_par", 16)))
            g_si = p.dma("gpsimd", "g_si", sin[:, :, :].rearrange("p a b -> p (a b)"),
                         sp_sstub.ap(), wait=_wl(g_so))

        evm = p.ev("vector", "tensor_scalar_mul", sin[:, :, :], sin[:, :, :],
                   smask_sb[:, 0:1], wait=_wl(g_si, ("g_msk", 16)))
        p.ev("vector", "memset", sin2b[:, :, :], 0.0)
        for ht in range(KC):
            p.ev("vector", "tensor_copy", sin2b[0:64, ht, 0:64], sin[0:64, ht, :])
            p.ev("vector", "tensor_copy", sin2b[64:128, ht, 64:128],
                 sin[64:128, ht, :])
        ev_sin = p.now("ev")

        str_c = sb("str_c", [P, 4, C], BF16)
        zt = sb("zt", [P, 2, 512], BF16)
        ea_scan = p.now("ea")
        evcg = {}
        for cg in range(4):
            gb = cg % 2
            d_c = p.dma("sync", "y_cc0", str_c[:, :, :],
                        sp_rt[cg * 512:(cg + 1) * 512, :]
                        .rearrange("(a p) c -> p a c", p=P),
                        wait=_wl(evcg.get(cg - 1), mm_chunk[NCH - 1]))
            for ht in range(KC):
                zb = 4 + (ht % 2)
                for a in range(4):
                    gtr = p.trans(psum_bf[0:P, zb * 1024 + a * P:zb * 1024 + (a + 1) * P],
                                  str_c[:, a, ht * P:(ht + 1) * P], identb,
                                  wait=_wl(d_c if a == 0 else None,
                                           bt.get(zb) if a == 0 else None))
                ze = p.ev("vector", "tensor_mul",
                          zt[:, gb, :].rearrange("p (a b) -> p a b", a=8),
                          psum_bf[0:P, zb * 1024:zb * 1024 + 512]
                          .rearrange("p (a b) -> p a b", a=8),
                          eofm[:, ht, cg * 8:(cg + 1) * 8].unsqueeze(2)
                          .broadcast_to([P, 8, L]),
                          wait=_wl(gtr, ev_sin if cg == 0 and ht == 0 else None,
                                   bt.get(100 + gb)))
                bt[zb] = ze
                cb2 = 6 + (ht % 2)
                gc = p.mm(bank(cb2), sin2b[:, ht, :], zt[:, gb, :],
                          wait=_wl(ze, bt.get(cb2)))
                ya = p.ev("vector", "tensor_add", y_full[:, ht, cg * 512:(cg + 1) * 512],
                          y_full[:, ht, cg * 512:(cg + 1) * 512], bank(cb2),
                          wait=_wl(gc, ea_scan if cg == 0 and ht == 0 else None))
                bt[cb2] = ya
                bt[100 + gb] = ya
            evcg[cg] = p.now("ev")
        freef("rT_t", "kT_t", "str_c", "zt", "s_st", "s2b", "sin", "sin2b",
              "smask_sb", "flag_sb", "fpoll_sb", "par_sb")

        ytok = sb("ytok", [P, C], F32)
        ysq = sb("ysq", [P, C], F32)
        vg_t = sb("vg_t", [P, 2, 2, C], BF16)
        gnst = sb("gnst", [P, 8, H], F32)
        lnw2 = sb("lnw2", [P, C], F32)
        lnb2 = sb("lnb2", [P, C], F32)
        yng_st = sb("yng_st", [P, 2, C], BF16)
        c_ln = p.dma("scalar", "c_ln2", lnw2, lnwrep_in)
        c_lnb = p.dma("scalar", "c_lnb", lnb2, lnbrep_in)
        dvg = {}
        evgn = {}
        dyng = {}
        gn_pg = {}
        for tt in range(TT):
            vb = tt % 2
            rows = slice(tt * P, (tt + 1) * P)
            d_v2 = p.dma("sync", f"y_gv{vb}", vg_t[:, vb, 0, :], sp_v[rows, :],
                         wait=_wl(evgn.get(tt - 2), gn_pg.get(tt - 2)))
            d_g2 = p.dma("sync", f"y_gg{vb}", vg_t[:, vb, 1, :], sp_g[rows, :])
            prev_tok = gn_pg.get(tt - 1)
            for ic in range(KC):
                tb2 = ic % 2
                gtr = p.trans(bank(tb2, cols=P), y_full[:, ic, rows],
                              identf,
                              wait=_wl(evcg[3] if tt == 0 and ic == 0 else None,
                                       bt.get(200 + tb2),
                                       prev_tok if ic == 0 else None))
                ce = p.ev("vector", "tensor_copy", ytok[:, ic * P:(ic + 1) * P],
                          bank(tb2, cols=P), wait=_wl(gtr))
                bt[200 + tb2] = ce
            p.ev("vector", "tensor_mul", ysq.rearrange("p (h n) -> p h n", h=H),
                 vg_t[:, vb, 0, :].rearrange("p (h n) -> p h n", h=H),
                 q_all[:, tt, :].unsqueeze(2).broadcast_to([P, H, N]),
                 wait=_wl(d_v2, ev_q if tt == 0 else None))
            ya2 = p.ev("vector", "tensor_add", ytok[:, :], ytok[:, :], ysq[:, :])
            p.ev("vector", "tensor_reduce", gnst[:, 0, :],
                 ytok.rearrange("p (h n) -> p h n", h=H), AX.X, ALU.add)
            sq = p.act(ysq[:, :], ytok[:, :], AF.Square, wait=_wl(ya2))
            p.ev("vector", "tensor_reduce", gnst[:, 1, :],
                 ysq.rearrange("p (h n) -> p h n", h=H), AX.X, ALU.add,
                 wait=_wl(sq))
            p.ev("vector", "tensor_scalar_mul", gnst[:, 2, :], gnst[:, 0, :],
                 1.0 / N)
            p.ev("vector", "tensor_scalar_mul", gnst[:, 3, :], gnst[:, 1, :],
                 1.0 / N)
            p.ev("vector", "tensor_mul", gnst[:, 4, :], gnst[:, 2, :],
                 gnst[:, 2, :])
            p.ev("vector", "tensor_sub", gnst[:, 5, :], gnst[:, 3, :],
                 gnst[:, 4, :])
            sqv = p.act(gnst[:, 6, :], gnst[:, 5, :], AF.Sqrt, bias=epsc[:, 0:1],
                        wait=_wl(p.now("ev")))
            p.ev("vector", "reciprocal", gnst[:, 7, :], gnst[:, 6, :],
                 wait=_wl(sqv))
            p.ev("vector", "tensor_sub", ytok.rearrange("p (h n) -> p h n", h=H),
                 ytok.rearrange("p (h n) -> p h n", h=H),
                 gnst[:, 2, :].unsqueeze(2).broadcast_to([P, H, N]))
            p.ev("vector", "tensor_mul", ytok.rearrange("p (h n) -> p h n", h=H),
                 ytok.rearrange("p (h n) -> p h n", h=H),
                 gnst[:, 7, :].unsqueeze(2).broadcast_to([P, H, N]))
            dvtok = p.now("ev")
            p.ev("gpsimd", "tensor_mul", ytok[:, :], ytok[:, :], lnw2[:, :],
                 wait=_wl(dvtok, c_ln if tt == 0 else None))
            p.ev("gpsimd", "tensor_add", ytok[:, :], ytok[:, :], lnb2[:, :],
                 wait=_wl(c_lnb if tt == 0 else None))
            ev_yng = p.ev("gpsimd", "tensor_mul", yng_st[:, vb, :], ytok[:, :],
                          vg_t[:, vb, 1, :],
                          wait=_wl(d_g2, dyng.get(tt - 2)))
            dyng[tt] = p.dma("sync", f"y_yo{vb}", sp_yng[rows, :],
                             yng_st[:, vb, :], wait=_wl(ev_yng))
            evgn[tt] = p.now("ev")
            gn_pg[tt] = p.now("pg")
        freef("ytok", "ysq", "vg_t", "gnst", "lnw2", "lnb2", "yng_st", "y_full",
              "alfm", "eofm", "q_all")

        if upto < 7:
            alldma = [p.now(s) for s in list(p.cnt) if s[:2] in ("y_", "g_", "c_")]
            _tail(p, out_t, x_in, wait=_wl(p.now("ev"), p.now("ea"), p.now("mm"),
                                           *alldma))
            p.finalize(ctx.enter_context(nc.Block()))
            return nc

        wo_sb = sb("wo_sb", [P, KC, C], BF16)
        yngt = sb("yngt", [P, 2, C], BF16)
        yngT = sb("yngT", [P, KC, P], BF16)
        xrt = sb("xrt", [P, 2, C], F32)
        resid = sb("resid", [P, C], F32)
        rsq = sb("rsq", [P, C], F32)
        lncol = sb("lncol", [P, 8], F32)
        xab = sb("xab", [P, C], BF16)
        xaT = sb("xaT", [P, KC, T], BF16)
        xa_st = sb("xa_st", [P, 2, C], F32)

        g_wo = p.dma("gpsimd", "g_wo", wo_sb,
                     wo_in.rearrange("(kc p) n -> p kc n", p=P),
                     wait=_wl(p.now("mm")))
        evwo = {}
        dxa = {}
        for tt in range(TT):
            vb = tt % 2
            rows = slice(tt * P, (tt + 1) * P)
            d_y = p.dma("sync", f"y_yi{vb}", yngt[:, vb, :], sp_yng[rows, :],
                        wait=_wl(dyng[tt], evwo.get(tt - 2)))
            d_x = p.dma("sync", f"y_xr{vb}", xrt[:, vb, :], x_in[rows, :])
            ttr = {}
            for ic in range(KC):
                tb2 = 4 + ic % 2
                gtr = p.trans(bank(tb2, dt=BF16, cols=P), yngt[:, vb, ic * P:(ic + 1) * P],
                              identb,
                              wait=_wl(d_y if ic == 0 else None, bt.get(300 + tb2)))
                ttr[ic] = p.act(yngT[:, ic, :], bank(tb2, dt=BF16, cols=P),
                                AF.Identity, wait=_wl(gtr))
                bt[300 + tb2] = ttr[ic]
            for n in range(4):
                ncs = slice(n * 512, (n + 1) * 512)
                ob = n % 4
                for jc in range(KC):
                    p.mm(bank(ob), yngT[:, jc, :], wo_sb[:, jc, ncs],
                         start=(jc == 0), stop=(jc == KC - 1),
                         wait=_wl(ttr[KC - 1] if jc == 0 else None,
                                  g_wo if tt == 0 and n == 0 and jc == 0 else None,
                                  bt.get(400 + ob) if jc == 0 else None))
                gw = p.now("mm")
                re = p.ev("vector", "tensor_add", resid[:, ncs], bank(ob),
                          xrt[:, vb, ncs], wait=_wl(gw, d_x if n == 0 else None))
                bt[400 + ob] = re
            rlast = p.ev("vector", "tensor_reduce", lncol[:, 0:1], resid[:, :],
                         AX.X, ALU.add)
            sq = p.act(rsq[:, :], resid[:, :], AF.Square, wait=_wl(rlast))
            p.ev("vector", "tensor_reduce", lncol[:, 1:2], rsq[:, :], AX.X,
                 ALU.add, wait=_wl(sq))
            p.ev("vector", "tensor_scalar_mul", lncol[:, 2:3], lncol[:, 0:1],
                 1.0 / C)
            p.ev("vector", "tensor_scalar_mul", lncol[:, 3:4], lncol[:, 1:2],
                 1.0 / C)
            p.ev("vector", "tensor_mul", lncol[:, 4:5], lncol[:, 2:3],
                 lncol[:, 2:3])
            p.ev("vector", "tensor_sub", lncol[:, 5:6], lncol[:, 3:4],
                 lncol[:, 4:5])
            sqv = p.act(lncol[:, 6:7], lncol[:, 5:6], AF.Sqrt, bias=epsc[:, 1:2],
                        wait=_wl(p.now("ev")))
            p.ev("vector", "reciprocal", lncol[:, 7:8], lncol[:, 6:7],
                 wait=_wl(sqv))
            exa = p.ev("vector", "tensor_scalar", xa_st[:, vb, :], resid[:, :],
                       lncol[:, 2:3], lncol[:, 7:8], ALU.subtract, ALU.mult,
                       wait=_wl(dxa.get(tt - 2)))
            dxa[tt] = p.dma("sync", f"y_xa{vb}", sp_xa[rows, :], xa_st[:, vb, :],
                            wait=_wl(exa))
            cb3 = p.ev("vector", "tensor_copy", xab[:, :], xa_st[:, vb, :])
            for ic in range(KC):
                tb2 = 6 + ic % 2
                gtr = p.trans(bank(tb2, dt=BF16, cols=P), xab[:, ic * P:(ic + 1) * P],
                              identb, wait=_wl(cb3 if ic == 0 else None,
                                               bt.get(500 + tb2)))
                ab = p.act(xaT[:, ic, rows], bank(tb2, dt=BF16, cols=P),
                           AF.Identity, wait=_wl(gtr))
                bt[500 + tb2] = ab
            evwo[tt] = p.now("ev")
        eawo = p.now("ea")
        freef("wo_sb", "yngt", "yngT", "xrt", "resid", "rsq", "lncol", "xab",
              "xa_st")

        if upto < 8:
            alldma = [p.now(s) for s in list(p.cnt) if s[:2] in ("y_", "g_", "c_")]
            _tail(p, out_t, x_in, wait=_wl(p.now("ev"), p.now("ea"), p.now("mm"),
                                           *alldma))
            p.finalize(ctx.enter_context(nc.Block()))
            return nc

        gate_sb = sb("gate_sb", [P, KC, NEXP], BF16)
        eb2 = sb("eb2", [P, NEXP], F32)
        sco = sb("sco", [P, 5, NEXP], F32)
        m12 = sb("m12", [P, 4], F32)
        gselb = sb("gselb", [P, NEXP], BF16)
        grows = sb("grows", [8, T], BF16)
        xaf = sb("xaf", [P, 2, C], F32)
        xlo = sb("xlo", [P, C], BF16)
        xloT = sb("xloT", [P, KC, P], BF16)

        c_gt = p.dma("gpsimd", "g_gt", gate_sb,
                     gate_in.rearrange("(kc p) e -> p kc e", p=P))
        c_eb = p.dma("scalar", "c_eb", eb2, ebrep_in)
        gtok = None
        ltok = None
        for tt in range(TT):
            tcols = slice(tt * P, (tt + 1) * P)
            fb7 = tt % 2
            dxa7 = p.dma("sync", f"y_x7{fb7}", xaf[:, fb7, :],
                         sp_xa[tt * P:(tt + 1) * P, :],
                         wait=_wl(dxa[tt] if tt < 2 else None, ltok))
            lc = p.ev("vector", "tensor_copy", xlo[:, :], xaf[:, fb7, :],
                      wait=_wl(dxa7))
            ltok = p.ev("vector", "tensor_sub", xlo[:, :], xaf[:, fb7, :],
                        xlo[:, :])
            ttr7 = None
            for ic in range(KC):
                tb7 = 6 + ic % 2
                gtr = p.trans(bank(tb7, dt=BF16, cols=P),
                              xlo[:, ic * P:(ic + 1) * P], identb,
                              wait=_wl(ltok if ic == 0 else None,
                                       bt.get(700 + tb7)))
                bt[700 + tb7] = ttr7 = p.ev(
                    "vector", "tensor_copy", xloT[:, ic, :],
                    bank(tb7, dt=BF16, cols=P), wait=_wl(gtr))
            for kc in range(KC):
                p.mm(bank(0, cols=NEXP), xaT[:, kc, tcols], gate_sb[:, kc, :],
                     start=(kc == 0), stop=False,
                     wait=_wl(c_gt if tt == 0 and kc == 0 else None,
                              gtok if kc == 0 else None))
            for kc in range(KC):
                p.mm(bank(0, cols=NEXP), xloT[:, kc, :], gate_sb[:, kc, :],
                     start=False, stop=(kc == KC - 1),
                     wait=_wl(ttr7 if kc == 0 else None))
            gmm = p.now("mm")
            p.ev("vector", "reduce_max", m12[:, 0:1], bank(0, cols=NEXP), AX.X,
                 wait=_wl(gmm))
            gtok = p.ev("vector", "tensor_scalar", sco[:, 0, :], bank(0, cols=NEXP),
                        m12[:, 0:1], None, ALU.subtract)
            exx = p.act(sco[:, 0, :], sco[:, 0, :], AF.Exp, wait=_wl(gtok))
            p.ev("vector", "tensor_reduce", m12[:, 1:2], sco[:, 0, :], AX.X,
                 ALU.add, wait=_wl(exx))
            p.ev("vector", "reciprocal", m12[:, 2:3], m12[:, 1:2])
            p.ev("vector", "tensor_scalar_mul", sco[:, 1, :], sco[:, 0, :],
                 m12[:, 2:3])
            p.ev("vector", "tensor_add", sco[:, 2, :], sco[:, 1, :], eb2[:, :],
                 wait=_wl(c_eb if tt == 0 else None))
            p.ev("vector", "reduce_max", m12[:, 0:1], sco[:, 2, :], AX.X)
            p.ev("vector", "tensor_scalar", sco[:, 3, :], sco[:, 2, :],
                 m12[:, 0:1], None, ALU.is_equal)
            p.ev("vector", "scalar_tensor_tensor", sco[:, 4, :], sco[:, 3, :],
                 1.0e30, sco[:, 2, :], ALU.mult, ALU.subtract)
            p.ev("vector", "tensor_reduce", m12[:, 1:2], sco[:, 4, :], AX.X,
                 ALU.min)
            p.ev("vector", "tensor_scalar_mul", m12[:, 2:3], m12[:, 1:2], -1.0)
            p.ev("vector", "tensor_scalar", sco[:, 4, :], sco[:, 2, :],
                 m12[:, 2:3], None, ALU.is_equal)
            p.ev("vector", "tensor_add", sco[:, 3, :], sco[:, 3, :], sco[:, 4, :])
            evg = p.ev("vector", "tensor_mul", gselb[:, :], sco[:, 1, :],
                       sco[:, 3, :])
            gt2 = p.trans(psum_bf[0:8, 1024:1024 + P], gselb[:, :], identb,
                          wait=_wl(evg, bt.get(600)))
            bt[600] = p.act(grows[0:8, tcols], psum_bf[0:8, 1024:1024 + P],
                            AF.Identity, wait=_wl(gt2))
        d_gr = p.dma("sync", "y_gw", sp_gr.ap(), grows[0:8, :], wait=_wl(bt[600]))

        xe = sb("xe", [P, KC, T], BF16)
        grep_t = sb("grep_t", [P, T], BF16)
        w1sl = sb("w1sl", [P, 2, 2, KC, P], BF16)
        silst = sb("silst", [P, 2, 512], F32)
        hstg = sb("hstg", [P, 4, 512], BF16)

        hs_tok = {}
        mm_hc = {}
        hmprev = {}
        p8_ev, p8_ea = p.now("ev"), p.now("ea")
        for E in range(9):
            first = (E == 0)
            if E < 8:
                dgr = p.dma("sync", f"y_ge{E % 2}", grep_t[:, :],
                            sp_gr.ap()[E:E + 1, :].to_broadcast([P, T]),
                            wait=_wl(d_gr if E < 2 else None,
                                     mm_hc.get(E - 2)))
                evx_kc = {}
                for kc in range(KC):
                    evx_kc[kc] = p.ev("vector", "tensor_mul", xe[:, kc, :],
                                      xaT[:, kc, :], grep_t[:, :],
                                      wait=_wl(dgr if kc == 0 else None,
                                               mm_hc.get(E - 1) if kc == 0 else None))
                xsrc = xe
            else:
                evx_kc = {}
                xsrc = xaT
            for hc in range(HC):
                hb = hc % 2
                hcols = slice(hc * P, (hc + 1) * P)
                wprev = (mm_hc.get((E, hc - 2)) if hc >= 2
                         else mm_hc.get((E - 1, HC - 2 + hc)))
                dW1 = p.dma("gpsimd", f"g_h1{hb}", w1sl[:, 0, hb, :, :],
                            fw1_in[E, :, hcols].rearrange("(kc p) h -> p kc h", p=P)
                            if E < 8 else
                            sw1_in[:, hcols].rearrange("(kc p) h -> p kc h", p=P),
                            wait=_wl(wprev))
                dW3 = p.dma("gpsimd", f"g_h3{hb}", w1sl[:, 1, hb, :, :],
                            fw3_in[E, :, hcols].rearrange("(kc p) h -> p kc h", p=P)
                            if E < 8 else
                            sw3_in[:, hcols].rearrange("(kc p) h -> p kc h", p=P))
                for tch in range(4):
                    tcs = slice(tch * 512, (tch + 1) * 512)
                    pa, pb = tch % 2, 2 + tch % 2
                    stg = tch % 4
                    w0 = _wl(dW1, dW3,
                             p8_ev if E == 0 and hc == 0 else None,
                             p8_ea if E == 0 and hc == 0 else None,
                             hs_tok.get((E, hc, tch - 2)) or
                             hs_tok.get((E, hc - 1, tch + 2)) or
                             hs_tok.get((E - 1, HC - 1, tch - 2)) or
                             (hs_tok.get((E - 1, HC - 1, tch + 2)) if tch < 2
                              else None))
                    for kc in range(KC):
                        wkc = list(w0 if kc == 0 else ())
                        if hc == 0 and tch == 0 and kc in evx_kc:
                            wkc.append(evx_kc[kc])
                        p.mm(bank(pa), w1sl[:, 0, hb, kc, :], xsrc[:, kc, tcs],
                             start=(kc == 0), stop=(kc == KC - 1), wait=wkc)
                        p.mm(bank(pb), w1sl[:, 1, hb, kc, :], xsrc[:, kc, tcs],
                             start=(kc == 0), stop=(kc == KC - 1))
                    ghm = p.now("mm")
                    gidx2 = (E * HC + hc) * 4 + tch
                    sa = p.act(silst[:, gidx2 % 2, :], bank(pa), AF.Silu,
                               wait=_wl(ghm, hmprev.get(gidx2 - 2)))
                    hm = p.ev("vector", "tensor_mul", hstg[:, stg, :],
                              silst[:, gidx2 % 2, :], bank(pb),
                              wait=_wl(sa, hs_tok.get((E, hc, tch - 4)) or
                                       hs_tok.get((E, hc - 1, tch)) or
                                       hs_tok.get((E - 1, HC - 1, tch))))
                    hmprev[gidx2] = hm
                    hs_tok[(E, hc, tch)] = p.dma(
                        "sync", f"y_hs{stg}",
                        sp_h[E, hc * P:(hc + 1) * P, tcs], hstg[:, stg, :],
                        wait=_wl(hm))
                mm_hc[(E, hc)] = p.now("mm")
            mm_hc[E] = p.now("mm")
        freef("xe", "grep_t", "w1sl", "silst", "hstg", "gate_sb", "eb2", "sco",
              "m12", "gselb", "grows", "xaT", "xaf", "xlo", "xloT")

        if upto < 9:
            alldma = [p.now(s) for s in list(p.cnt) if s[:2] in ("y_", "g_", "c_")]
            _tail(p, out_t, x_in, wait=_wl(p.now("ev"), p.now("ea"), p.now("mm"),
                                           *alldma))
            p.finalize(ctx.enter_context(nc.Block()))
            return nc

        HHC = HC // 2
        acc = sb("acc", [P, TT, C], F32)
        w2h = sb("w2h", [P, 2, HHC, C], BF16)
        hti2 = sb("hti2", [P, 2, HHC, P], BF16)

        acc_tok = {}
        mm_eh = {}
        mm_tt9 = {}
        for e in range(9):
            for hh in range(2):
                g2 = e * 2 + hh
                wb = g2 % 2
                hrows = slice(hh * 896, (hh + 1) * 896)
                srcw = (fw2_in[e, hrows, :] if e < 8 else sw2_in[hrows, :])
                dw2 = p.dma("gpsimd", f"g_wh{wb}", w2h[:, wb, :, :],
                            srcw.rearrange("(hc p) n -> p hc n", p=P),
                            wait=_wl(mm_eh.get(g2 - 2)))
                for tt in range(TT):
                    hb = tt % 2
                    trows = slice(tt * P, (tt + 1) * P)
                    wde = []
                    if hh == 0 and tt == 0:
                        wde = [hs_tok.get((e, HC - 1, s)) for s in range(4)]
                    dh = p.dma("sync", f"y_h2{hb}", hti2[:, hb, :, :],
                               sp_h[e, hrows, trows]
                               .rearrange("(hc p) t -> p hc t", p=P),
                               wait=_wl(mm_tt9.get((g2, tt - 2)) or
                                        mm_tt9.get((g2 - 1, TT - 2 + tt)),
                                        *wde))
                    for cq in range(4):
                        ccols = slice(cq * 512, (cq + 1) * 512)
                        ob = (tt * 4 + cq) % 4
                        for hc in range(HHC):
                            p.mm(bank(ob), hti2[:, hb, hc, :],
                                 w2h[:, wb, hc, ccols],
                                 start=(hc == 0), stop=(hc == HHC - 1),
                                 wait=_wl(dh if cq == 0 else None,
                                          dw2 if tt == 0 else None,
                                          acc_tok.get((g2, tt * 4 + cq - 4)) or
                                          acc_tok.get((g2 - 1, 60 + tt * 4 + cq)))
                                 if hc == 0 else ())
                        gmm9 = p.now("mm")
                        if g2 == 0:
                            acc_tok[(g2, tt * 4 + cq)] = p.ev(
                                "vector", "tensor_copy", acc[:, tt, ccols],
                                bank(ob), wait=_wl(gmm9))
                        else:
                            acc_tok[(g2, tt * 4 + cq)] = p.ev(
                                "vector", "tensor_add", acc[:, tt, ccols],
                                acc[:, tt, ccols], bank(ob), wait=_wl(gmm9))
                    mm_tt9[(g2, tt)] = p.now("mm")
                mm_eh[g2] = p.now("mm")
        freef("w2h", "hti2")

        xat2 = sb("xat2", [P, 2, C], F32)
        res2 = sb("res2", [P, C], F32)
        r2sq = sb("r2sq", [P, C], F32)
        lnc2 = sb("lnc2", [P, 8], F32)
        ostf = sb("ostf", [P, 2, C], F32)

        sttok = {}
        evfin = {}
        for tt in range(TT):
            fb = tt % 2
            trows = slice(tt * P, (tt + 1) * P)
            d_xa2 = p.dma("sync", f"y_fx{fb}", xat2[:, fb, :], sp_xa[trows, :],
                          wait=_wl(dxa[tt] if tt < 2 else None,
                                   evfin.get(tt - 2)))
            ra = p.ev("vector", "tensor_add", res2[:, :], acc[:, tt, :],
                      xat2[:, fb, :],
                      wait=_wl(d_xa2, acc_tok[(17, tt * 4 + 3)]))
            rl = p.ev("vector", "tensor_reduce", lnc2[:, 0:1], res2[:, :], AX.X,
                      ALU.add)
            sq2 = p.act(r2sq[:, :], res2[:, :], AF.Square, wait=_wl(rl))
            p.ev("vector", "tensor_reduce", lnc2[:, 1:2], r2sq[:, :], AX.X,
                 ALU.add, wait=_wl(sq2))
            p.ev("vector", "tensor_scalar_mul", lnc2[:, 2:3], lnc2[:, 0:1],
                 1.0 / C)
            p.ev("vector", "tensor_scalar_mul", lnc2[:, 3:4], lnc2[:, 1:2],
                 1.0 / C)
            p.ev("vector", "tensor_mul", lnc2[:, 4:5], lnc2[:, 2:3],
                 lnc2[:, 2:3])
            p.ev("vector", "tensor_sub", lnc2[:, 5:6], lnc2[:, 3:4],
                 lnc2[:, 4:5])
            sqv2 = p.act(lnc2[:, 6:7], lnc2[:, 5:6], AF.Sqrt, bias=epsc[:, 1:2],
                         wait=_wl(p.now("ev"), ev_eps if tt == 0 else None))
            p.ev("vector", "reciprocal", lnc2[:, 7:8], lnc2[:, 6:7],
                 wait=_wl(sqv2))
            evfin[tt] = p.ev("vector", "tensor_scalar", ostf[:, fb, :],
                             res2[:, :], lnc2[:, 2:3], lnc2[:, 7:8],
                             ALU.subtract, ALU.mult,
                             wait=_wl(sttok.get(tt - 2)))
            sttok[tt] = p.dma("sync", f"st{fb}", out_t[trows, :], ostf[:, fb, :],
                              wait=_wl(evfin[tt]))

        alldma = [p.now(s) for s in list(p.cnt) if s[:2] in ("y_", "g_", "c_", "st")]
        p.emit("sync", lambda E: E.nop(),
               wait=_wl(p.now("ev"), p.now("ea"), p.now("mm"), p.now("pg"),
                        *alldma))
        p.finalize(ctx.enter_context(nc.Block()))
        return nc


def _tail(p, out_t, x_src, wait=()):
    st = p.dma("sync", "st", out_t, x_src, wait=wait)
    p.emit("sync", lambda E: E.nop(), wait=[st])


_NC_CACHE = {}


def _bd_tri_np(n, blocks, strict):
    b = np.triu(np.ones((n, n), np.float32), 1 if strict else 0)
    out = np.zeros((n * blocks, n * blocks), np.float32)
    for i in range(blocks):
        out[i * n:(i + 1) * n, i * n:(i + 1) * n] = b
    return out


def _consts():
    c = {}
    c["l2i"] = _bd_tri_np(64, 2, strict=False)
    c["l2x"] = _bd_tri_np(64, 2, strict=True)
    c["l32x"] = np.triu(np.ones((32, 32), np.float32), 1)
    msu = np.triu(np.ones((L, L), np.float32), 1)
    c["masksu"] = np.concatenate([msu, msu], 0)
    ch = np.zeros((P, 2), np.float32)
    ch[0:64, 0] = 1.0
    ch[64:128, 1] = 1.0
    c["chones"] = ch
    c["identf"] = np.eye(P, dtype=np.float32)
    c["identb"] = np.eye(P, dtype=np.float32)
    return c


def make_in_maps(inputs, nonce=None):
    g = lambda k: np.asarray(inputs[k], np.float32)
    x = g("L0_attn_x")
    consts = _consts()
    if nonce is None:
        nonce = int(np.random.randint(1, 2 ** 30))

    def col16(v):
        return np.ascontiguousarray(np.asarray(v, np.float32).reshape(-1).reshape(KC, P).T)

    maas = [col16(g("L0_attn_time_maa_" + s)) for s in "xwkvrg"]
    maatab = np.ascontiguousarray(np.stack(maas, 1))

    shared = {
        "tm_w1": g("L0_attn_w1"), "tm_w2": g("L0_attn_w2"),
        "td_w1": g("L0_attn_td_w1"), "td_w2": g("L0_attn_td_w2"),
        "wr": g("L0_attn_wr"), "wk": g("L0_attn_wk"), "wv": g("L0_attn_wv"),
        "wg": g("L0_attn_wg"), "wo": g("L0_attn_wo"),
        "gate_w": g("L0_ffn_gate_w"),
        "fw1": g("L0_ffn_w1"), "fw2": g("L0_ffn_w2"), "fw3": g("L0_ffn_w3"),
        "sw1": g("L0_ffn_sw1"), "sw2": g("L0_ffn_sw2"), "sw3": g("L0_ffn_sw3"),
        "maatab": maatab,
        "urep": np.ascontiguousarray(
            np.broadcast_to(g("L0_attn_time_faaaa").reshape(-1), (P, C))),
        "lnwrep": np.ascontiguousarray(
            np.broadcast_to(g("L0_attn_ln_w").reshape(-1), (P, C))),
        "lnbrep": np.ascontiguousarray(
            np.broadcast_to(g("L0_attn_ln_b").reshape(-1), (P, C))),
        "tdrep": np.ascontiguousarray(
            np.broadcast_to(g("L0_attn_time_decay").reshape(-1), (P, C))),
        "ebrep": np.ascontiguousarray(
            np.broadcast_to(g("L0_ffn_expert_bias").reshape(-1), (P, NEXP))),
        **consts,
    }
    in_maps = []
    for core in range(8):
        b, j = core // 2, core % 2
        sl = np.ascontiguousarray(x[b, j * T:(j + 1) * T])
        xp = np.zeros(C, np.float32) if j == 0 else np.asarray(x[b, j * T - 1], np.float32)
        m = dict(shared)
        m["x"] = sl
        m["xprev"] = col16(xp)
        m["parity"] = np.array([[j]], np.int32)
        m["smask"] = np.full((128, 1), float(j), np.float32)
        m["nonce"] = np.array([[nonce]], np.int32)
        in_maps.append(m)
    return in_maps


def kernel(**inputs):
    if "full" not in _NC_CACHE:
        _NC_CACHE["full"] = build_program()
    nc = _NC_CACHE["full"]
    in_maps = make_in_maps(inputs)
    res = run_bass_kernel_spmd(nc, in_maps, core_ids=list(range(8)))
    outs = [res.results[i]["out"] for i in range(8)]
    full = np.stack(outs, 0).reshape(4, 2, T, C).reshape(4, 4096, C)
    return np.ascontiguousarray(full.astype(np.float32))
